# version 02
# baseline (speedup 1.0000x reference)
import numpy as np
import ml_dtypes
import concourse.bass as bass
import concourse.mybir as mybir
from concourse.bass_utils import run_bass_kernel_spmd
from contextlib import ExitStack

F32 = mybir.dt.float32
BF16 = mybir.dt.bfloat16
AF = mybir.ActivationFunctionType
ALU = mybir.AluOpType
AX = mybir.AxisListType

RC = 1.5 * 2 ** 23
EPS = 1e-5
NCORES = 8
NB = 2
S = 2048
D = 1024
HID = 4096
NT = S // 128
NDT = D // 128
NCT = HID // 128
WIN = 256
NWIN = S // WIN
QB = 512
NQB = S // QB
NKT = S // 128
NHG = 2
FPG = 4


class _Op:
    __slots__ = ("eng", "fn", "waits", "flag", "dma_sem", "post", "val",
                 "pos", "know", "pre_drain")

    def __init__(self, eng, fn, waits):
        self.eng = eng
        self.fn = fn
        self.waits = [w for w in waits if w is not None]
        self.flag = False
        self.dma_sem = None
        self.post = 0
        self.val = 0
        self.pos = 0
        self.know = {}
        self.pre_drain = False


class _Sem:
    __slots__ = ("name", "handle", "count")

    def __init__(self, name):
        self.name = name
        self.handle = None
        self.count = 0


class Trace:
    ENGS = ("sync", "gpsimd", "tensor", "vector", "scalar")

    def __init__(self):
        self.ops = []
        self.esem = {e: _Sem("es_" + e) for e in self.ENGS}
        self.dsems = {}
        self.eng_pos = {e: 0 for e in self.ENGS}
        self.eng_know = {e: {} for e in self.ENGS}
        self.n_drains = 0

    def dma_sem(self, name):
        if name not in self.dsems:
            self.dsems[name] = _Sem("ds_" + name)
        return self.dsems[name]

    def add(self, eng, fn, waits=(), sdeps=()):
        op = _Op(eng, fn, waits)
        self.eng_pos[eng] += 1
        op.pos = self.eng_pos[eng]
        know = dict(self.eng_know[eng])
        sdeps = list(sdeps)

        def join(prod):
            for k, v in prod.know.items():
                if know.get(k, -1) < v:
                    know[k] = v
            if know.get(prod.eng, -1) < prod.pos:
                know[prod.eng] = prod.pos

        for w in op.waits:
            if w[0] == "e":
                if w[1].eng == eng:
                    sdeps.append(w[1])
                else:
                    join(w[1])
            elif w[0] == "d" and len(w) > 3 and w[3] is not None:
                join(w[3])
        need = any(d is not None and d.pos > know.get(eng, 0) for d in sdeps)
        if need:
            op.pre_drain = True
            self.n_drains += 1
            know[eng] = op.pos - 1
        op.know = know
        self.eng_know[eng] = know
        self.ops.append(op)
        return op

    def dma(self, eng, fn, sem, waits=(), sdeps=()):
        w = [x for x in waits if x is not None]
        if sem.count > 0:
            w.append(("d", sem, sem.count * 16, None))
        op = self.add(eng, fn, w, sdeps)
        sem.count += 1
        op.dma_sem = sem
        op.post = sem.count * 16
        return op

    @staticmethod
    def ev(op):
        if op is None:
            return None
        if op.dma_sem is not None:
            return ("d", op.dma_sem, op.post, op)
        op.flag = True
        return ("e", op)

    def emit(self, nc, ctx):
        cnt = {e: 0 for e in self.esem}
        for op in self.ops:
            if op.dma_sem is None and op.flag:
                cnt[op.eng] += 1
                op.val = cnt[op.eng]
        for s in list(self.esem.values()) + list(self.dsems.values()):
            s.handle = ctx.enter_context(nc.semaphore(s.name))
        block = ctx.enter_context(nc.Block())

        def make_body(ename):
            my = [op for op in self.ops if op.eng == ename]

            def body(e):
                waited = {}
                for op in my:
                    if op.pre_drain:
                        e.drain()
                    for w in op.waits:
                        if w[0] == "d":
                            sem, val = w[1], w[2]
                        else:
                            if w[1].eng == ename:
                                continue
                            sem, val = self.esem[w[1].eng], w[1].val
                        if waited.get(id(sem), -1) < val:
                            waited[id(sem)] = val
                            e.wait_ge(sem.handle, val)
                    ins = op.fn(e)
                    if op.dma_sem is not None:
                        ins.then_inc(op.dma_sem.handle, 16)
                    elif op.flag:
                        ins.then_inc(self.esem[op.eng].handle, 1)
            return body

        for ename in self.ENGS:
            if any(op.eng == ename for op in self.ops):
                getattr(block, ename)(make_body(ename))


E = Trace.ev


def build_program(phases=("f1", "at", "f2"), nb=NB):
    nc = bass.Bass("TRN2", target_bir_lowering=False, debug=False)

    x_in = nc.dram_tensor("x", [nb * S, D], F32, kind="ExternalInput").ap()
    w_qkv = nc.dram_tensor("w_qkv", [D, 3 * D], F32, kind="ExternalInput").ap()
    bqcol = nc.dram_tensor("bqcol", [128, 24], F32, kind="ExternalInput").ap()
    bqv_row = nc.dram_tensor("bqv_row", [1, D], F32, kind="ExternalInput").ap()
    bout_row = nc.dram_tensor("bout_row", [1, D], F32, kind="ExternalInput").ap()
    w_out = nc.dram_tensor("w_out", [D, D], F32, kind="ExternalInput").ap()
    w1 = nc.dram_tensor("w1", [HID, D], F32, kind="ExternalInput").ap()
    cw_in = nc.dram_tensor("cw_pt", [128, 3 * NCT], F32, kind="ExternalInput").ap()
    cb_in = nc.dram_tensor("cb_pt", [128, NCT], F32, kind="ExternalInput").ap()
    w2 = nc.dram_tensor("w2", [D, HID], F32, kind="ExternalInput").ap()
    y_out = nc.dram_tensor("y", [nb * S, D], F32, kind="ExternalOutput").ap()

    wq1t = nc.dram_tensor("wq1t_scr", [NCT, 128, D], BF16, kind="Internal").ap()
    wq2t = nc.dram_tensor("wq2t_scr", [NCT, 128, D], BF16, kind="Internal").ap()
    ident = nc.inline_tensor(np.eye(128, dtype=np.float32), name="identf32").ap()

    tr = Trace()

    with ExitStack() as ctx:
        def sb(name, shape, dtype):
            return ctx.enter_context(nc.sbuf_tensor(name, shape, dtype))

        X = sb("X", [128, NT * D], F32)
        CW = sb("CW", [128, NCT * 3], F32)
        CB = sb("CB", [128, NCT], F32)
        BQCOL = sb("BQCOL", [128, 24], F32)
        IDF = sb("IDF", [128, 128], F32)
        ONEF = sb("ONEF", [128, 1], F32)
        NEGONE = sb("NEGONE", [128, 1], F32)
        EPSC = sb("EPSC", [128, 1], F32)
        ONEFR = sb("ONEFR", [1, 128], F32)
        ONEBR = sb("ONEBR", [1, 128], BF16)
        ONEBC = sb("ONEBC", [128, 1], BF16)
        RCCOL = sb("RCCOL", [128, 1], F32)
        NRCCOL = sb("NRCCOL", [128, 1], F32)
        WSC = sb("WSC", [128, 4], F32)
        BQVB = sb("BQVB", [128, D], F32)
        BOUTB = sb("BOUTB", [128, D], F32)
        STATS = sb("STATS", [128, 64], F32)
        AMAX1 = sb("AMAX1", [128, NT], F32)
        S1C = sb("S1C", [128, NT], F32)
        F1C = sb("F1C", [128, NT], F32)
        AMAX2 = sb("AMAX2", [128, NT], F32)
        S2CC = sb("S2CC", [128, NT], F32)
        FAC2 = sb("FAC2", [128, NT], F32)
        SCR16 = sb("SCR16", [128, NT], F32)
        F1ROW = sb("F1ROW", [1, S], F32)
        S2ROW = sb("S2ROW", [1, WIN], F32)
        ROWS = sb("ROWS", [1, 64], F32)

        AR_A = sb("AR_A", [128, NDT * S], BF16)
        AR_B = sb("AR_B", [128, NCT * WIN], F32)
        AR_C = sb("AR_C", [128, NCT * WIN], BF16)
        WQ1B = sb("WQ1B", [128, 4 * D], BF16)
        WQ2B = sb("WQ2B", [128, 8 * D], BF16)
        HS = sb("HS", [128, 2 * (WIN + 2)], F32)
        TA = sb("TA", [128, 2 * WIN], F32)
        TB = sb("TB", [128, 2 * WIN], F32)
        F1B = sb("F1B", [128, WIN + 2], F32)
        S2B = sb("S2B", [128, WIN], F32)
        ACC = sb("ACC", [128, WIN], F32)
        QR = sb("QR", [128, D], F32)
        BST = sb("BST", [128, 12], F32)
        AVT = sb("AVT", [128, FPG * QB], BF16)
        NEGB = sb("NEGB", [128, 8], F32)
        BCS = TA.bitcast(F32)[0:64, 0:QB]
        AVST = HS.bitcast(BF16)[0:64, 0:QB]
        QSQ = F1B.bitcast(BF16)[:, 0:QB]
        ROWR = TB[0:1, 0:QB]
        ROWB = ACC.bitcast(BF16)[0:1, 0:QB]

        LN1 = AR_C.bitcast(F32)
        XQ = AR_B.bitcast(BF16)
        H2 = AR_B
        XQT = AR_A
        XQ2 = AR_C
        L2T = AR_A
        L2TOK = AR_B.bitcast(BF16)
        ARBB = AR_B.bitcast(BF16)
        KT = AR_C
        VBLK = 8 * 65
        E_OFF = NKT * VBLK
        QT_OFF = E_OFF + 4096

        PS1 = ctx.enter_context(nc.psum_tensor("PS1", [128, 1024], F32))
        PS2 = ctx.enter_context(nc.psum_tensor("PS2", [128, 1024], F32))
        PA = ctx.enter_context(nc.psum_tensor("PA", [128, 512], F32))
        PB = ctx.enter_context(nc.psum_tensor("PB", [128, 512], F32))
        PC = ctx.enter_context(nc.psum_tensor("PC", [128, 512], F32))
        PD = ctx.enter_context(nc.psum_tensor("PD", [128, 512], F32))

        sem_x = tr.dma_sem("xld")
        sem_y = tr.dma_sem("yst")
        sem_m1 = [tr.dma_sem("wq1_%d" % i) for i in range(3)]
        sem_m2 = [tr.dma_sem("wq2_%d" % i) for i in range(3)]
        sem_tp = [tr.dma_sem("tp_%d" % i) for i in range(2)]
        sem_wl = [tr.dma_sem("wl_%d" % i) for i in range(4)]
        sem_sw = [tr.dma_sem("sw_%d" % i) for i in range(2)]
        sem_qk = [tr.dma_sem("qk_%d" % i) for i in range(2)]
        sem_sh = tr.dma_sem("shift")

        tr.add("vector", lambda e: e.memset(ONEF[:], 1.0))
        tr.add("vector", lambda e: e.memset(NEGONE[:], -1.0))
        tr.add("vector", lambda e: e.memset(EPSC[:], EPS))
        tr.add("vector", lambda e: e.memset(ONEFR[:], 1.0))
        tr.add("vector", lambda e: e.memset(ONEBR[:], 1.0))
        tr.add("vector", lambda e: e.memset(ONEBC[:], 1.0))
        tr.add("vector", lambda e: e.memset(RCCOL[:], RC))
        op_consts = tr.add("vector", lambda e: e.memset(NRCCOL[:], -RC))

        ld_id = tr.dma("sync", lambda e: e.dma_start(IDF[:], ident), sem_wl[0])
        ld_cw = tr.dma("sync", lambda e: e.dma_start(CW[:], cw_in), sem_wl[0])
        ld_cb = tr.dma("sync", lambda e: e.dma_start(CB[:], cb_in), sem_wl[0])
        ld_bq = tr.dma("sync", lambda e: e.dma_start(BQCOL[:], bqcol), sem_wl[0])
        ld_bv = tr.dma("sync", lambda e: e.dma_start(F1ROW[0:1, 0:D], bqv_row), sem_wl[0])
        ld_bo = tr.dma("sync", lambda e: e.dma_start(F1ROW[0:1, D:2 * D], bout_row), sem_wl[0])

        bc_prev = None
        for (dst, base) in ((BQVB, 0), (BOUTB, D)):
            for c0 in (0, 512):
                m = tr.add("tensor", lambda e, base=base, c0=c0: e.matmul(
                    PC[:, 0:512], ONEFR[0:1, :], F1ROW[0:1, base + c0:base + c0 + 512],
                    start=True, stop=True),
                    [E(ld_bv), E(ld_bo), E(op_consts), E(bc_prev)])
                bc_prev = tr.add("scalar", lambda e, dst=dst, c0=c0: e.copy(
                    dst[:, c0:c0 + 512], PC[:, 0:512]), [E(m)],
                    sdeps=[bc_prev])
        op_bias_b = bc_prev

        W1LD = AR_B
        WQBF = AR_C
        STG = AR_A
        PSUMW = STATS

        def prep_matrix(src_ap, nblk, blk_cols, nslot, ws_col, inv_col, inv_scale,
                        scratch, is_w1, rows_c0, prev_events):
            reds = []
            for i in range(nblk):
                w = list(prev_events) if i < nslot else [E(reds[i - nslot])]
                q = "sync" if i % 2 == 0 else "gpsimd"
                ld = tr.dma(q, lambda e, i=i, nslot=nslot: e.dma_start(
                    W1LD[:, (i % nslot) * blk_cols:(i % nslot + 1) * blk_cols],
                    src_ap.rearrange("(n p) c -> n p c", p=128)[i]),
                    sem_wl[i % len(sem_wl)], w)
                red = tr.add("vector", lambda e, i=i, nslot=nslot: e.tensor_reduce(
                    PSUMW[:, i:i + 1],
                    W1LD[:, (i % nslot) * blk_cols:(i % nslot + 1) * blk_cols],
                    AX.X, ALU.add, apply_absolute_value=True), [E(ld)])
                reds.append(red)
            redall = tr.add("vector", lambda e: e.tensor_reduce(
                PSUMW[:, 40:41], PSUMW[:, 0:nblk], AX.X, ALU.add), [],
                sdeps=[reds[-1]])
            msum = tr.add("tensor", lambda e: e.matmul(
                PC[0:1, 0:1], PSUMW[:, 40:41], ONEF[:, 0:1], start=True, stop=True),
                [E(redall)] + list(prev_events))
            c0 = rows_c0
            t1 = tr.add("scalar", lambda e: e.copy(ROWS[0:1, c0:c0 + 1], PC[0:1, 0:1]),
                        [E(msum)])
            t2 = tr.add("vector", lambda e: e.tensor_scalar(
                ROWS[0:1, c0 + 1:c0 + 2], ROWS[0:1, c0:c0 + 1], 1.0 / float(HID * D),
                1e-5, ALU.mult, ALU.max), [E(t1)])
            t3 = tr.add("vector", lambda e: e.reciprocal(
                ROWS[0:1, c0 + 2:c0 + 3], ROWS[0:1, c0 + 1:c0 + 2]), [], sdeps=[t2])
            t4 = tr.add("vector", lambda e: e.reciprocal(
                ROWS[0:1, c0 + 3:c0 + 4], ROWS[0:1, c0 + 2:c0 + 3]), [], sdeps=[t3])
            t4b = tr.add("vector", lambda e: e.tensor_scalar(
                ROWS[0:1, c0 + 4:c0 + 5], ROWS[0:1, c0 + 3:c0 + 4], inv_scale, None,
                ALU.mult), [], sdeps=[t4])
            mws = tr.add("tensor", lambda e: e.matmul(
                PC[:, 0:1], ONEFR[0:1, :], ROWS[0:1, c0 + 2:c0 + 3], start=True,
                stop=True), [E(t4b), E(t1)])
            cws = tr.add("scalar", lambda e: e.copy(
                WSC[:, ws_col:ws_col + 1], PC[:, 0:1]), [E(mws)], sdeps=[t1])
            minv = tr.add("tensor", lambda e: e.matmul(
                PC[:, 1:2], ONEFR[0:1, :], ROWS[0:1, c0 + 4:c0 + 5], start=True,
                stop=True), [E(cws)])
            cinv = tr.add("scalar", lambda e: e.copy(
                WSC[:, inv_col:inv_col + 1], PC[:, 1:2]), [E(minv)], sdeps=[cws])
            prev_q = {}
            stores = []
            tps = []
            for i in range(nblk):
                wslot = (i % nslot) * blk_cols
                q = "sync" if i % 2 == 0 else "gpsimd"
                ld = tr.dma(q, lambda e, wslot=wslot, i=i: e.dma_start(
                    W1LD[:, wslot:wslot + blk_cols],
                    src_ap.rearrange("(n p) c -> n p c", p=128)[i]),
                    sem_wl[i % len(sem_wl)],
                    [E(prev_q.get(i % nslot, redall))])
                q1 = tr.add("vector", lambda e, wslot=wslot: e.tensor_scalar(
                    W1LD[:, wslot:wslot + blk_cols], W1LD[:, wslot:wslot + blk_cols],
                    WSC[:, ws_col:ws_col + 1], RC, ALU.mult, ALU.add),
                    [E(ld), E(cinv)] + (list(prev_events) if i == 0 else []))
                q2 = tr.add("scalar", lambda e, wslot=wslot: e.activation(
                    W1LD[:, wslot:wslot + blk_cols], W1LD[:, wslot:wslot + blk_cols],
                    AF.Identity, bias=NRCCOL[:, 0:1]), [E(q1)])
                q3 = tr.add("vector", lambda e, wslot=wslot: e.tensor_scalar(
                    WQBF[:, wslot:wslot + blk_cols], W1LD[:, wslot:wslot + blk_cols],
                    -1.0, 1.0, ALU.max, ALU.min),
                    [E(q2)] + (list(prev_events) if i == 0 else [])
                    + ([E(tps[i - nslot])] if i >= nslot else []))
                prev_q[i % nslot] = q3
                ntile = blk_cols // 128
                tp = tr.dma("scalar", lambda e, wslot=wslot, ntile=ntile:
                            e.dma_start_transpose(
                    STG[:, 0:blk_cols].rearrange("p (n f) -> p n f", n=ntile),
                    WQBF[:, wslot:wslot + blk_cols].rearrange("p (n f) -> p n f",
                                                              n=ntile)),
                    sem_tp[i % 2],
                    [E(q3)] + ([E(stores[-1])] if stores else list(prev_events)))
                tps.append(tp)
                if is_w1:
                    stx = tr.dma("sync", lambda e, i=i: e.dma_start(
                        scratch[i], STG[:, 0:blk_cols]), sem_sw[0], [E(tp)])
                else:
                    stx = tr.dma("sync", lambda e, i=i: e.dma_start(
                        scratch[:, :, i * 128:(i + 1) * 128]
                        .rearrange("ct p fm -> p ct fm"),
                        STG[:, 0:blk_cols].rearrange("p (ct fm) -> p ct fm", fm=128)),
                        sem_sw[1], [E(tp)])
                stores.append(stx)
            return stores, q3, cinv

        w1_stores, w1_lastq, w1_cinv = prep_matrix(
            w1, NCT, D, 4, 2, 0, 1.0, wq1t, True, 8, [E(ld_id), E(op_bias_b)])
        w2_stores, w2_lastq, w2_cinv = prep_matrix(
            w2, NDT, HID, 2, 3, 1, 0.5, wq2t, False, 16,
            [E(w1_lastq), E(w1_stores[-1])])
        w1_done = ("d", tr.dsems["sw_0"], tr.dsems["sw_0"].count * 16, w1_stores[-1])
        w2_done = ("d", tr.dsems["sw_1"], tr.dsems["sw_1"].count * 16, w2_stores[-1])
        prep_done = [w1_done, w2_done, E(w2_lastq)]

        def layernorm_tile(tt, out_ap, waits, sdeps=()):
            s1 = tr.add("vector", lambda e, tt=tt: e.bn_stats(
                BST[:, 0:6], X[:, tt * D:tt * D + 512]), waits, sdeps=sdeps)
            tr.add("vector", lambda e, tt=tt: e.bn_stats(
                BST[:, 6:12], X[:, tt * D + 512:tt * D + 1024]), [])
            ag = tr.add("vector", lambda e, tt=tt: e.bn_aggr(
                STATS[:, 2 * tt:2 * tt + 2], BST[:]), [], sdeps=[s1])
            t1 = tr.add("scalar", lambda e, tt=tt: e.activation(
                STATS[:, 32 + tt:33 + tt], STATS[:, 2 * tt + 1:2 * tt + 2],
                AF.Identity, bias=EPSC[:, 0:1]), [E(ag)])
            t2 = tr.add("vector", lambda e, tt=tt: e.reciprocal(
                STATS[:, 48 + tt:49 + tt], STATS[:, 32 + tt:33 + tt]), [E(t1)],
                sdeps=[ag])
            t3 = tr.add("scalar", lambda e, tt=tt: e.activation(
                STATS[:, 32 + tt:33 + tt], STATS[:, 48 + tt:49 + tt], AF.Sqrt),
                [E(t2)], sdeps=[t1])
            t4 = tr.add("vector", lambda e, tt=tt: e.scalar_tensor_tensor(
                STATS[:, 48 + tt:49 + tt], STATS[:, 2 * tt:2 * tt + 1],
                NEGONE[:, 0:1], STATS[:, 32 + tt:33 + tt], ALU.mult, ALU.mult),
                [E(t3)], sdeps=[t2])
            ap_op = tr.add("scalar", lambda e, tt=tt, out_ap=out_ap: e.activation(
                out_ap, X[:, tt * D:(tt + 1) * D], AF.Identity,
                bias=STATS[:, 48 + tt:49 + tt], scale=STATS[:, 32 + tt:33 + tt]),
                [E(t4)], sdeps=[t3])
            return ap_op

        ffn_state = {"m1dma": {}, "m2dma": {}, "m1cur": None, "m2cur": None}

        def ffn(x_ready, arena_free, xread_sdep):
            pass

            round_ops = []
            prev_r1 = None
            for tt in range(NT):
                lnslot = (tt % 2) * D
                lnw = list(x_ready) + list(arena_free) if tt == 0 else []
                ap_op = layernorm_tile(
                    tt, LN1[:, lnslot:lnslot + D], lnw,
                    sdeps=([round_ops[-1]] if tt >= 2 else
                           ([xread_sdep] if tt == 0 else [])))
                am = tr.add("vector", lambda e, tt=tt, lnslot=lnslot: e.tensor_reduce(
                    AMAX1[:, tt:tt + 1], LN1[:, lnslot:lnslot + D], AX.X, ALU.max,
                    apply_absolute_value=True), [E(ap_op)])
                c1 = tr.add("vector", lambda e, tt=tt: e.tensor_scalar(
                    SCR16[:, tt:tt + 1], AMAX1[:, tt:tt + 1], 1e-5, None, ALU.max),
                    [], sdeps=[am])
                c2 = tr.add("vector", lambda e, tt=tt: e.reciprocal(
                    S1C[:, tt:tt + 1], SCR16[:, tt:tt + 1]), [], sdeps=[c1])
                c3 = tr.add("vector", lambda e, tt=tt: e.tensor_scalar(
                    S1C[:, tt:tt + 1], S1C[:, tt:tt + 1], 127.0, None, ALU.mult),
                    [], sdeps=[c2])
                r1 = tr.add("scalar", lambda e, tt=tt, lnslot=lnslot: e.activation(
                    QR[:], LN1[:, lnslot:lnslot + D], AF.Identity,
                    bias=RCCOL[:, 0:1], scale=S1C[:, tt:tt + 1]),
                    [E(c3), E(op_consts)], sdeps=[prev_r1])
                prev_r1 = r1
                r2 = tr.add("vector", lambda e, tt=tt: e.tensor_scalar(
                    XQ[:].rearrange("p (dt n c) -> p dt n c", dt=NDT, n=NT)[:, :, tt, :],
                    QR[:].rearrange("p (dt c) -> p dt c", c=128), -RC, None, ALU.add),
                    [E(r1)] + (list(arena_free) if tt == 0 else []))
                round_ops.append(r2)
            f1a = tr.add("vector", lambda e: e.reciprocal(F1C[:], S1C[:]), [],
                         sdeps=[round_ops[-1]])
            f1b_op = tr.add("vector", lambda e: e.tensor_scalar(
                F1C[:], F1C[:], WSC[:, 0:1], None, ALU.mult),
                [w1_done, E(w1_cinv)], sdeps=[f1a])
            f1row_ops = []
            prev_cp = None
            for tt in range(NT):
                tp = tr.add("tensor", lambda e, tt=tt: e.transpose(
                    PD[0:1, 0:128], F1C[:, tt:tt + 1], IDF[:]),
                    [E(f1b_op), E(ld_id), E(prev_cp)])
                prev_cp = tr.add("scalar", lambda e, tt=tt: e.copy(
                    F1ROW[0:1, tt * 128:(tt + 1) * 128], PD[0:1, 0:128]), [E(tp)])
                f1row_ops.append(prev_cp)
            tp_ops = []
            for dt in range(NDT):
                tp_ops.append(tr.dma("scalar", lambda e, dt=dt: e.dma_start_transpose(
                    XQT[:, dt * S:(dt + 1) * S].rearrange("p (n f) -> p n f", n=NT),
                    XQ[:, dt * S:(dt + 1) * S].rearrange("p (n f) -> p n f", n=NT)),
                    sem_tp[dt % 2], [E(round_ops[-1])] + list(arena_free)))
            xqt_done = [E(t) for t in tp_ops]

            st = {"evac": {}, "silu_last": {}, "ra_last": {}, "round_last": {},
                  "m2stop": {}, "evr": {}, "sbc": None, "fcp": None,
                  "red_prev": None, "s2rowcp": None, "g_last": None}
            all_evr = []

            def emit_m1_header(w):
                cin0 = max(WIN * w - 1, 0)
                wid = min(WIN * w + WIN + 1, S) - cin0
                fm = tr.add("tensor", lambda e, cin0=cin0, wid=wid: e.matmul(
                    PC[:, 0:wid], ONEFR[0:1, :], F1ROW[0:1, cin0:cin0 + wid],
                    start=True, stop=True),
                    [E(f1row_ops[-1]), E(st["sbc"])])
                st["fcp"] = tr.add("scalar", lambda e, wid=wid: e.copy(
                    F1B[:, 0:wid], PC[:, 0:wid]), [E(fm)],
                    sdeps=[st["s2rowcp"]])

            def emit_m1_ct(w, ct):
                cin0 = max(WIN * w - 1, 0)
                wid = min(WIN * w + WIN + 1, S) - cin0
                off = WIN * w - cin0
                jl0 = 1 if w == 0 else 0
                jr1 = WIN - 1 if w == NWIN - 1 else WIN
                g = ct // 2
                if ct % 2 == 0:
                    bufsel = g % 2
                    dw = [ffn_state["m1dma"].get(bufsel)]
                    if w == 0 and ct == 0:
                        dw += [w1_done] + list(x_ready)
                    ffn_state["m1cur"] = tr.dma(
                        "sync", lambda e, g=g, bufsel=bufsel: e.dma_start(
                            WQ1B[:, bufsel * 2 * D:(bufsel + 1) * 2 * D]
                            .rearrange("p (ct c) -> p ct c", c=D),
                            wq1t[2 * g:2 * g + 2].rearrange("ct p c -> p ct c")),
                        sem_m1[g % 3], dw)
                cur = ffn_state["m1cur"]
                lct = ct % 2
                bufsel = g % 2
                pslot = (ct % 2) * 512
                mm = None
                for dt in range(NDT):
                    waits = []
                    if dt == 0:
                        waits = [("d", cur.dma_sem, cur.post, cur),
                                 E(st["evac"].get(pslot))]
                        if w == 0 and ct <= 1:
                            waits = waits + xqt_done
                    mm = tr.add("tensor", lambda e, dt=dt, cin0=cin0, wid=wid,
                                pslot=pslot, bufsel=bufsel, lct=lct: e.matmul(
                        PS1[:, pslot:pslot + wid],
                        WQ1B[:, bufsel * 2 * D + lct * D + dt * 128:
                             bufsel * 2 * D + lct * D + (dt + 1) * 128],
                        XQT[:, dt * S + cin0:dt * S + cin0 + wid],
                        start=(dt == 0), stop=(dt == NDT - 1)), waits)
                if ct % 2 == 1:
                    ffn_state["m1dma"][bufsel] = E(mm)
                hslot = (ct % 2) * (WIN + 2)
                ev = tr.add("vector", lambda e, wid=wid, pslot=pslot, hslot=hslot:
                            e.tensor_mul(HS[:, hslot:hslot + wid],
                                         PS1[:, pslot:pslot + wid], F1B[:, 0:wid]),
                            [E(mm), E(st["fcp"]),
                             E(st["silu_last"].get(w - 1 if ct <= 1 else w))])
                st["evac"][pslot] = ev
                taslot = (ct % 2) * WIN
                cwc = ct * 3
                tR = tr.add("scalar", lambda e, hslot=hslot, taslot=taslot,
                            jr1=jr1, off=off, cwc=cwc: e.activation(
                    TA[:, taslot:taslot + jr1],
                    HS[:, hslot + off + 1:hslot + off + 1 + jr1],
                    AF.Identity, scale=CW[:, cwc + 2:cwc + 3]),
                    [E(ev), E(ld_cw)],
                    sdeps=[st["silu_last"].get(w - 1 if ct <= 1 else w)])
                after_tr = [tR]
                if jr1 < WIN:
                    z = tr.add("vector", lambda e, taslot=taslot, jr1=jr1: e.memset(
                        TA[:, taslot + jr1:taslot + WIN], 0.0), [E(tR)])
                    after_tr.append(z)
                tL = tr.add("vector", lambda e, hslot=hslot, taslot=taslot,
                            jl0=jl0, off=off, cwc=cwc: e.scalar_tensor_tensor(
                    TB[:, taslot + jl0:taslot + WIN],
                    HS[:, hslot + off + jl0 - 1:hslot + off + WIN - 1],
                    CW[:, cwc:cwc + 1],
                    TA[:, taslot + jl0:taslot + WIN], ALU.mult, ALU.add),
                    [E(o) for o in after_tr])
                zb = None
                if jl0 > 0:
                    zb = tr.add("vector", lambda e, taslot=taslot: e.tensor_copy(
                        TB[:, taslot:taslot + 1], TA[:, taslot:taslot + 1]),
                        [E(tR)], sdeps=[tL])
                tM = tr.add("gpsimd", lambda e, hslot=hslot, taslot=taslot,
                            off=off, cwc=cwc: e.tensor_tensor(
                    TA[:, taslot:taslot + WIN], HS[:, hslot + off:hslot + off + WIN],
                    CW[:, cwc + 1:cwc + 2].broadcast_to([128, WIN]), ALU.mult),
                    [E(tL), E(zb)], sdeps=[st["g_last"]])
                yy = tr.add("gpsimd", lambda e, taslot=taslot: e.tensor_add(
                    TB[:, taslot:taslot + WIN], TB[:, taslot:taslot + WIN],
                    TA[:, taslot:taslot + WIN]), [], sdeps=[tM])
                st["g_last"] = yy
                sl = tr.add("scalar", lambda e, ct=ct, taslot=taslot: e.activation(
                    H2[:, ct * WIN:(ct + 1) * WIN], TB[:, taslot:taslot + WIN],
                    AF.Silu, bias=CB[:, ct:ct + 1]),
                    [E(yy), E(ld_cb)],
                    sdeps=[st["ra_last"].get(w - 1), tR])
                st["silu_last"][w] = sl

            def emit_m2_ct(w, ct):
                psg = [PS2[:, 0:512], PS2[:, 512:1024], PA[:], PB[:]]
                g = ct // 2
                if ct % 2 == 0:
                    bufsel = g % 2
                    dw = [ffn_state["m2dma"].get(bufsel)]
                    if w == 0 and ct == 0:
                        dw += [w2_done] + list(x_ready)
                    ffn_state["m2cur"] = tr.dma(
                        "sync", lambda e, g=g, bufsel=bufsel: e.dma_start(
                            WQ2B[:, bufsel * 2 * D:(bufsel + 1) * 2 * D]
                            .rearrange("p (ct c) -> p ct c", c=D),
                            wq2t[2 * g:2 * g + 2].rearrange("ct p c -> p ct c")),
                        sem_m2[g % 3], dw)
                cur = ffn_state["m2cur"]
                lct = ct % 2
                bufsel = g % 2
                mm2 = None
                for tt in range(2):
                    for fb in range(2):
                        waits = []
                        if tt == 0 and fb == 0:
                            waits = [("d", cur.dma_sem, cur.post, cur)]
                            if ct == 0:
                                waits += [E(st["round_last"][w])]
                                waits += [E(o) for o in st["evr"].get(w - 2, [])]
                        mm2 = tr.add("tensor", lambda e, ct=ct, tt=tt, fb=fb,
                                     bufsel=bufsel, lct=lct: e.matmul(
                            psg[tt * 2 + fb],
                            XQ2[:, ct * WIN + tt * 128:ct * WIN + (tt + 1) * 128],
                            WQ2B[:, bufsel * 2 * D + lct * D + fb * 512:
                                 bufsel * 2 * D + lct * D + (fb + 1) * 512],
                            start=(ct == 0), stop=(ct == NCT - 1)), waits)
                if ct % 2 == 1:
                    ffn_state["m2dma"][bufsel] = E(mm2)
                if ct == NCT - 1:
                    st["m2stop"][w] = mm2

            def emit_m2_evacs(w):
                psg = [PS2[:, 0:512], PS2[:, 512:1024], PA[:], PB[:]]
                evrs = []
                prev = None
                for tt in range(2):
                    for fb in range(2):
                        gtt = 2 * w + tt
                        evr = tr.add("vector", lambda e, tt=tt, fb=fb, gtt=gtt:
                                     e.scalar_tensor_tensor(
                            X[:, gtt * D + fb * 512:gtt * D + (fb + 1) * 512],
                            psg[tt * 2 + fb], FAC2[:, gtt:gtt + 1],
                            X[:, gtt * D + fb * 512:gtt * D + (fb + 1) * 512],
                            ALU.mult, ALU.add),
                            [E(st["m2stop"][w]), E(st["fc2_%d" % w])],
                            sdeps=[st["fc2_%d" % w]] if prev is None else [])
                        prev = evr
                        evrs.append(evr)
                st["evr"][w] = evrs
                all_evr.extend(evrs)

            def emit_tail(w):
                red = tr.add("vector", lambda e: e.tensor_reduce(
                    ACC[:], H2[:].rearrange("p (ct j) -> p j ct", j=WIN), AX.X,
                    ALU.max, apply_absolute_value=True),
                    [E(st["silu_last"][w])], sdeps=[st["red_prev"]])
                st["red_prev"] = red
                prev = red
                for i in range(2):
                    tp = tr.add("tensor", lambda e, i=i: e.transpose(
                        PD[:, 0:128], ACC[:, i * 128:(i + 1) * 128], IDF[:]),
                        [E(prev)])
                    rd = tr.add("vector", lambda e, i=i, w=w: e.tensor_reduce(
                        AMAX2[:, 2 * w + i:2 * w + i + 1], PD[:, 0:128], AX.X,
                        ALU.max), [E(tp)], sdeps=[red])
                    prev = rd
                s2a = tr.add("vector", lambda e, w=w: e.tensor_scalar(
                    SCR16[:, 0:2], AMAX2[:, 2 * w:2 * w + 2], 1e-5, None, ALU.max),
                    [], sdeps=[prev])
                s2b_ = tr.add("vector", lambda e, w=w: e.reciprocal(
                    S2CC[:, 2 * w:2 * w + 2], SCR16[:, 0:2]), [], sdeps=[s2a])
                s2c = tr.add("vector", lambda e, w=w: e.tensor_scalar(
                    S2CC[:, 2 * w:2 * w + 2], S2CC[:, 2 * w:2 * w + 2], 127.0, None,
                    ALU.mult), [], sdeps=[s2b_])
                fc1 = tr.add("vector", lambda e, w=w: e.reciprocal(
                    FAC2[:, 2 * w:2 * w + 2], S2CC[:, 2 * w:2 * w + 2]), [],
                    sdeps=[s2c])
                st["fc2_%d" % w] = tr.add("vector", lambda e, w=w: e.tensor_scalar(
                    FAC2[:, 2 * w:2 * w + 2], FAC2[:, 2 * w:2 * w + 2], WSC[:, 1:2],
                    None, ALU.mult), [w2_done, E(w2_cinv)], sdeps=[fc1])
                rowcp = st["s2rowcp"]
                for i in range(2):
                    tps2 = tr.add("tensor", lambda e, i=i, w=w: e.transpose(
                        PD[0:1, 0:128], S2CC[:, 2 * w + i:2 * w + i + 1], IDF[:]),
                        [E(s2c), E(rowcp)])
                    rowcp = tr.add("scalar", lambda e, i=i: e.copy(
                        S2ROW[0:1, i * 128:(i + 1) * 128], PD[0:1, 0:128]),
                        [E(tps2)], sdeps=[st["fcp"]] if i == 0 else [])
                st["s2rowcp"] = rowcp
                sbm = tr.add("tensor", lambda e: e.matmul(
                    PC[:, 0:WIN], ONEFR[0:1, :], S2ROW[0:1, 0:WIN], start=True,
                    stop=True), [E(rowcp), E(st["fcp"])])
                sbc = tr.add("scalar", lambda e: e.copy(S2B[:], PC[:, 0:WIN]),
                             [E(sbm)], sdeps=[rowcp])
                st["sbc"] = sbc
                ra = None
                rbs = []
                for ct in range(NCT):
                    qs = (ct % 2) * WIN
                    mu = tr.add("gpsimd", lambda e, ct=ct: e.tensor_mul(
                        H2[:, ct * WIN:(ct + 1) * WIN],
                        H2[:, ct * WIN:(ct + 1) * WIN], S2B[:]),
                        [E(sbc)] if ct == 0 else [],
                        sdeps=[st["g_last"]] if ct == 0 else [])
                    if ct == 0:
                        st["g_last"] = None
                    ra = tr.add("scalar", lambda e, ct=ct, qs=qs: e.activation(
                        QR[:, qs:qs + WIN], H2[:, ct * WIN:(ct + 1) * WIN],
                        AF.Identity, bias=RCCOL[:, 0:1]),
                        [E(mu)] + ([E(st["m2stop"].get(w - 1))] if ct == 0 else [])
                        + ([E(rbs[ct - 2])] if ct >= 2 else []),
                        sdeps=[st["silu_last"][w] if ct == 0 else None])
                    rb = tr.add("vector", lambda e, ct=ct, qs=qs:
                                e.tensor_scalar(
                        XQ2[:, ct * WIN:(ct + 1) * WIN], QR[:, qs:qs + WIN],
                        -RC, None, ALU.add), [E(ra)])
                    rbs.append(rb)
                st["ra_last"][w] = ra
                st["round_last"][w] = rbs[-1]

            for w in range(NWIN + 1):
                if w < NWIN:
                    emit_m1_header(w)
                    for ct in range(NCT):
                        emit_m1_ct(w, ct)
                if w >= 1:
                    for ct in range(NCT):
                        emit_m2_ct(w - 1, ct)
                    emit_m2_evacs(w - 1)
                if w < NWIN:
                    emit_tail(w)
            return all_evr

        at_state = {"exp_prev": None, "av_prev": None, "pair_evacs": None,
                    "pe_prev": None, "bc_prev": None, "shift_prev": None,
                    "sq_guard": None, "s2_guard": None, "mx_prev": None,
                    "rowb_guard": None, "xread_guard": None}

        def attn(x_ready, arena_free, xread_sdep):
            ln_last = None
            for tt in range(NT):
                ln_last = layernorm_tile(
                    tt, L2TOK[:].rearrange("p (dt n c) -> p dt n c", dt=NDT, n=NT)
                    [:, :, tt, :],
                    (list(x_ready) + list(arena_free)) if tt == 0 else [],
                    sdeps=[xread_sdep] if tt == 0 else [])
            ltp = []
            for dt in range(NDT):
                ltp.append(tr.dma("scalar", lambda e, dt=dt: e.dma_start_transpose(
                    L2T[:, dt * S:(dt + 1) * S].rearrange("p (n f) -> p n f", n=NT),
                    L2TOK[:, dt * S:(dt + 1) * S].rearrange("p (n f) -> p n f", n=NT)),
                    sem_tp[dt % 2], [E(ln_last)] + list(arena_free)))
            l2t_done = [E(t) for t in ltp]

            preadd = []
            pa_prev = None
            for tt in range(NT):
                pa_ = tr.add("vector", lambda e, tt=tt: e.tensor_add(
                    X[:, tt * D:(tt + 1) * D], X[:, tt * D:(tt + 1) * D], BOUTB[:]),
                    [E(ln_last), E(op_bias_b)] if tt == 0 else [],
                    sdeps=[xread_sdep] if tt == 0 else [])
                preadd.append(pa_)
                pa_prev = pa_

            last_proj = []
            for hg in range(NHG):
                guard = l2t_done if hg == 0 else [E(o) for o in last_proj[-8:]]
                kt_last_evac = None
                ksq_prev = None
                kw_consumed = None
                k2m_init = tr.add("vector", lambda e: e.memset(ROWS[0:1, 0:16], 0.0),
                                  guard,
                                  sdeps=[at_state["mx_prev"], pa_prev])
                for fp in range(FPG):
                    pair = hg * FPG + fp
                    kw = tr.dma("gpsimd", lambda e, pair=pair: e.dma_start(
                        WQ1B[:, 0:D].rearrange("p (dt c) -> p dt c", c=128),
                        w_qkv.rearrange("(dt p) c -> p dt c", p=128)
                        [:, :, D + pair * 128:D + (pair + 1) * 128]), sem_qk[0],
                        guard + [E(kw_consumed)])
                    km = None
                    for Nb in range(4):
                        for dt in range(NDT):
                            waits = []
                            if dt == 0:
                                waits = [("d", kw.dma_sem, kw.post, kw),
                                         E(kt_last_evac)]
                                if Nb == 0 and fp == 0:
                                    waits = waits + guard
                            km = tr.add("tensor", lambda e, fp=fp, Nb=Nb, dt=dt:
                                        e.matmul(
                                PC[:, 0:512], WQ1B[:, dt * 128:(dt + 1) * 128],
                                L2T[:, dt * S + Nb * 512:dt * S + (Nb + 1) * 512],
                                start=(dt == 0), stop=(dt == NDT - 1)), waits)
                        ke = tr.add("scalar", lambda e, fp=fp, Nb=Nb, pair=pair:
                                    e.activation(
                            KT[:, fp * S + Nb * 512:fp * S + (Nb + 1) * 512],
                            PC[:, 0:512], AF.Identity,
                            bias=BQCOL[:, 8 + pair:9 + pair]), [E(km), E(ld_bq)])
                        kt_last_evac = ke
                        sq = tr.add("vector", lambda e, fp=fp, Nb=Nb: e.tensor_mul(
                            QSQ[:], KT[:, fp * S + Nb * 512:fp * S + (Nb + 1) * 512],
                            KT[:, fp * S + Nb * 512:fp * S + (Nb + 1) * 512]),
                            [E(ke)], sdeps=[at_state["sq_guard"]])
                        prev_rd = None
                        for hh in range(2):
                            qm = tr.add("tensor", lambda e, hh=hh: e.matmul(
                                PD[0:1, 0:512], ONEBC[hh * 64:hh * 64 + 64, 0:1],
                                QSQ[hh * 64:hh * 64 + 64, :], start=True, stop=True),
                                [E(sq), E(at_state["pe_prev"]), E(prev_rd)])
                            rd = tr.add("vector", lambda e: e.tensor_reduce(
                                ROWS[0:1, 32:33], PD[0:1, 0:512], AX.X, ALU.max),
                                [E(qm)], sdeps=[sq, ksq_prev])
                            prev_rd = rd
                            mx = tr.add("vector", lambda e, fp=fp, hh=hh:
                                        e.tensor_tensor(
                                ROWS[0:1, 2 * fp + hh:2 * fp + hh + 1],
                                ROWS[0:1, 2 * fp + hh:2 * fp + hh + 1],
                                ROWS[0:1, 32:33], ALU.max), [E(k2m_init)],
                                sdeps=[rd])
                            ksq_prev = mx
                            at_state["sq_guard"] = mx
                    kw_consumed = ke
                at_state["mx_prev"] = ksq_prev
                mk1 = tr.add("scalar", lambda e: e.activation(
                    ROWS[0:1, 16:24], ROWS[0:1, 0:8], AF.Ln), [E(ksq_prev)])
                mk2 = tr.add("scalar", lambda e: e.activation(
                    ROWS[0:1, 16:24], ROWS[0:1, 16:24], AF.Exp, scale=0.5), [],
                    sdeps=[mk1])

                vw = tr.dma("gpsimd", lambda e, hg=hg: e.dma_start(
                    WQ2B[:, 0:4 * D].rearrange("p (dt c) -> p dt c", c=512),
                    w_qkv.rearrange("(dt p) c -> p dt c", p=128)
                    [:, :, 2 * D + hg * 512:2 * D + (hg + 1) * 512]), sem_qk[1],
                    [E(kt_last_evac)] + guard)
                vones = tr.add("vector", lambda e: e.memset(
                    ARBB[:, 0:NKT * VBLK].rearrange("p (t c) -> p t c", c=VBLK)
                    [:, :, 64:VBLK:65], 1.0), guard, sdeps=[at_state["mx_prev"]])
                v_evacs = []
                for tt in range(NKT):
                    vm = None
                    for dt in range(NDT):
                        waits = []
                        if dt == 0:
                            waits = [("d", vw.dma_sem, vw.post, vw), E(vones),
                                     E(ksq_prev)]
                            if tt >= 2:
                                waits.append(E(v_evacs[tt - 2]))
                        vm = tr.add("tensor", lambda e, tt=tt, dt=dt: e.matmul(
                            PC[:, 0:512] if tt % 2 == 0 else PD[:, 0:512],
                            L2T[:, dt * S + tt * 128:dt * S + (tt + 1) * 128],
                            WQ2B[:, dt * 512:(dt + 1) * 512],
                            start=(dt == 0), stop=(dt == NDT - 1)), waits)
                    ve = tr.add("vector", lambda e, tt=tt, hg=hg: e.tensor_add(
                        ARBB[:, tt * VBLK:(tt + 1) * VBLK]
                        .rearrange("p (h c) -> p h c", c=65)[:, :, 0:64],
                        (PC[:, 0:512] if tt % 2 == 0 else PD[:, 0:512])
                        .rearrange("p (h c) -> p h c", c=64),
                        BQVB[:, hg * 512:(hg + 1) * 512]
                        .rearrange("p (h c) -> p h c", c=64)), [E(vm)],
                        sdeps=[vones] if tt == 0 else [])
                    v_evacs.append(ve)

                qw_consumed = None
                wo_dma = None
                for qb in range(NQB):
                    qt_last = None
                    negb_last = None
                    for fp in range(FPG):
                        pair = hg * FPG + fp
                        qw = tr.dma("gpsimd", lambda e, pair=pair: e.dma_start(
                            WQ1B[:, D:2 * D].rearrange("p (dt c) -> p dt c", c=128),
                            w_qkv.rearrange("(dt p) c -> p dt c", p=128)
                            [:, :, pair * 128:(pair + 1) * 128]), sem_qk[0],
                            [E(qw_consumed), E(v_evacs[-1])])
                        qm = None
                        for dt in range(NDT):
                            waits = []
                            if dt == 0:
                                waits = [("d", qw.dma_sem, qw.post, qw), E(qt_last),
                                         E(v_evacs[-1]), E(at_state["pe_prev"])]
                            qm = tr.add("tensor", lambda e, fp=fp, dt=dt, qb=qb:
                                        e.matmul(
                                PC[:, 0:512], WQ1B[:, D + dt * 128:D + (dt + 1) * 128],
                                L2T[:, dt * S + qb * 512:dt * S + (qb + 1) * 512],
                                start=(dt == 0), stop=(dt == NDT - 1)), waits)
                        qw_consumed = qm
                        qe = tr.add("scalar", lambda e, fp=fp, pair=pair: e.activation(
                            ARBB[:, QT_OFF + fp * 512:QT_OFF + (fp + 1) * 512],
                            PC[:, 0:512], AF.Identity,
                            bias=BQCOL[:, pair:pair + 1]), [E(qm)])
                        qt_last = qe
                        sq = tr.add("vector", lambda e, fp=fp: e.tensor_mul(
                            QSQ[:], ARBB[:, QT_OFF + fp * 512:QT_OFF + (fp + 1) * 512],
                            ARBB[:, QT_OFF + fp * 512:QT_OFF + (fp + 1) * 512]),
                            [E(qe)], sdeps=[at_state["sq_guard"]])
                        prev_rd = None
                        for hh in range(2):
                            qm2 = tr.add("tensor", lambda e, hh=hh: e.matmul(
                                PD[0:1, 0:512], ONEBC[hh * 64:hh * 64 + 64, 0:1],
                                QSQ[hh * 64:hh * 64 + 64, :], start=True, stop=True),
                                [E(sq), E(at_state["pe_prev"]), E(prev_rd),
                                 E(at_state["s2_guard"])])
                            rd = tr.add("vector", lambda e: e.tensor_reduce(
                                ROWS[0:1, 33:34], PD[0:1, 0:512], AX.X, ALU.max),
                                [E(qm2)], sdeps=[sq])
                            prev_rd = rd
                            at_state["sq_guard"] = rd
                            s1_ = tr.add("scalar", lambda e: e.activation(
                                ROWS[0:1, 34:35], ROWS[0:1, 33:34], AF.Ln), [E(rd)],
                                sdeps=[at_state["s2_guard"]])
                            s2_ = tr.add("scalar", lambda e: e.activation(
                                ROWS[0:1, 34:35], ROWS[0:1, 34:35], AF.Exp,
                                scale=0.5), [], sdeps=[s1_])
                            at_state["s2_guard"] = s2_
                            b1 = tr.add("vector", lambda e, fp=fp, hh=hh: e.tensor_mul(
                                ROWS[0:1, 40 + 2 * fp + hh:41 + 2 * fp + hh],
                                ROWS[0:1, 34:35],
                                ROWS[0:1, 16 + 2 * fp + hh:17 + 2 * fp + hh]),
                                [E(s2_), E(mk2)], sdeps=[rd])
                            negb_last = tr.add("vector", lambda e, fp=fp, hh=hh:
                                               e.tensor_scalar(
                                ROWS[0:1, 40 + 2 * fp + hh:41 + 2 * fp + hh],
                                ROWS[0:1, 40 + 2 * fp + hh:41 + 2 * fp + hh],
                                -0.125, None, ALU.mult), [], sdeps=[b1])
                    nbm = tr.add("tensor", lambda e: e.matmul(
                        PC[:, 0:8], ONEFR[0:1, :], ROWS[0:1, 40:48], start=True,
                        stop=True), [E(negb_last), E(qt_last)])
                    nbc = tr.add("scalar", lambda e: e.copy(NEGB[:], PC[:, 0:8]),
                                 [E(nbm)], sdeps=[qt_last])

                    for fp in range(FPG):
                        av_last = {0: None, 1: None}
                        exp_ops = {}
                        av = None
                        for g in range(8):
                            sc_last = None
                            for kk in range(2):
                                kt = 2 * g + kk
                                for hh in range(2):
                                    waits = []
                                    if kk == 0 and hh == 0:
                                        if g >= 1:
                                            waits = [E(exp_ops[(g - 1, 0)]),
                                                     E(exp_ops[(g - 1, 1)])]
                                        else:
                                            waits = [E(nbc), E(at_state["exp_prev"])]
                                    sc_last = tr.add("tensor", lambda e, fp=fp,
                                                     kt=kt, hh=hh, kk=kk: e.matmul(
                                        (PS1 if hh == 0 else PS2)
                                        [:, kk * 512:(kk + 1) * 512],
                                        KT[hh * 64:hh * 64 + 64,
                                           fp * S + kt * 128:fp * S + (kt + 1) * 128],
                                        ARBB[hh * 64:hh * 64 + 64,
                                             QT_OFF + fp * 512:QT_OFF + (fp + 1) * 512],
                                        start=True, stop=True,
                                        tile_position=(hh * 64, 0)), waits)
                            eset = g % 2
                            for hh in range(2):
                                ew = [E(sc_last)]
                                if g >= 2:
                                    ew.append(E(av_last[eset]))
                                elif at_state["av_prev"] is not None:
                                    ew.append(at_state["av_prev"])
                                exp_ops[(g, hh)] = tr.add(
                                    "scalar", lambda e, fp=fp, hh=hh, eset=eset:
                                    e.activation(
                                        ARBB[:, E_OFF + eset * 2048 + hh * 1024:
                                             E_OFF + eset * 2048 + (hh + 1) * 1024],
                                        (PS1 if hh == 0 else PS2)[:],
                                        AF.Exp, scale=0.125,
                                        bias=NEGB[:, 2 * fp + hh:2 * fp + hh + 1]),
                                    ew, sdeps=[nbc] if (g == 0 and hh == 0) else [])
                            for kk in range(2):
                                kt = 2 * g + kk
                                for hh in range(2):
                                    waits = []
                                    if kk == 0:
                                        waits = [E(exp_ops[(g, 0)]),
                                                 E(exp_ops[(g, 1)])]
                                        if g == 0 and hh == 0 and at_state["pair_evacs"]:
                                            waits += at_state["pair_evacs"]
                                    av = tr.add("tensor", lambda e, fp=fp, kt=kt,
                                                hh=hh, eset=eset, kk=kk: e.matmul(
                                        (PA if hh == 0 else PB)[0:65, :],
                                        ARBB[:, kt * VBLK + (2 * fp + hh) * 65:
                                             kt * VBLK + (2 * fp + hh) * 65 + 65],
                                        ARBB[:, E_OFF + eset * 2048 + hh * 1024 + kk * 512:
                                             E_OFF + eset * 2048 + hh * 1024 + (kk + 1) * 512],
                                        start=(kt == 0), stop=(kt == NKT - 1)), waits)
                            av_last[eset] = av
                        at_state["exp_prev"] = exp_ops[(7, 1)]
                        at_state["av_prev"] = E(av)

                        pair_evacs = []
                        shift_ev = None
                        for hh in range(2):
                            ps = PA if hh == 0 else PB
                            rc_ = tr.add("vector", lambda e, ps=ps: e.reciprocal(
                                ROWR[0:1, :], ps[64:65, :]), [E(av)],
                                sdeps=[at_state["rowb_guard"]])
                            rb_ = tr.add("scalar", lambda e: e.copy(
                                ROWB[0:1, :], ROWR[0:1, :]), [E(rc_)],
                                sdeps=[at_state["bc_prev"]])
                            bm = tr.add("tensor", lambda e: e.matmul(
                                PC[0:64, :], ONEBR[0:1, 0:64], ROWB[0:1, :],
                                start=True, stop=True),
                                [E(rb_), E(at_state["bc_prev"])])
                            bc_ = tr.add("scalar", lambda e: e.copy(
                                BCS[:], PC[0:64, :]), [E(bm)], sdeps=[rb_])
                            at_state["bc_prev"] = bc_
                            if hh == 0:
                                mu = tr.add("vector", lambda e, fp=fp, ps=ps:
                                            e.tensor_mul(
                                    AVT[0:64, fp * QB:(fp + 1) * QB], ps[0:64, :],
                                    BCS[:]), [E(bc_)], sdeps=[rc_])
                            else:
                                mu = tr.add("vector", lambda e, ps=ps: e.tensor_mul(
                                    AVST[:], ps[0:64, :], BCS[:]),
                                    [E(bc_), at_state["shift_prev"]],
                                    sdeps=[rc_])
                                shift_ev = tr.dma("gpsimd", lambda e, fp=fp:
                                                  e.dma_start(
                                    AVT[64:128, fp * QB:(fp + 1) * QB], AVST[:]),
                                    sem_sh, [E(mu)])
                                at_state["shift_prev"] = (
                                    "d", shift_ev.dma_sem, shift_ev.post, shift_ev)
                            at_state["rowb_guard"] = bc_
                            pair_evacs.append(E(mu))
                        if shift_ev is not None:
                            pair_evacs.append(("d", shift_ev.dma_sem, shift_ev.post,
                                               shift_ev))
                        at_state["pair_evacs"] = pair_evacs

                    if qb == 0:
                        wo_dma = tr.dma("gpsimd", lambda e, hg=hg: e.dma_start(
                            WQ2B[:, 4 * D:8 * D].rearrange("p (dt c) -> p dt c", c=D),
                            w_out.rearrange("(dt p) c -> p dt c", p=128)
                            [:, hg * 4:hg * 4 + 4, :]), sem_qk[1],
                            [E(v_evacs[-1])])
                    for tt in range(4):
                        for fb in range(2):
                            pm = None
                            for fp in range(FPG):
                                waits = []
                                if fp == 0:
                                    waits = [("d", wo_dma.dma_sem, wo_dma.post, wo_dma),
                                             E(at_state["pe_prev"])]
                                    if tt == 0 and fb == 0:
                                        waits += at_state["pair_evacs"]
                                pm = tr.add("tensor", lambda e, tt=tt, fb=fb, fp=fp:
                                            e.matmul(
                                    PD[:, 0:512],
                                    AVT[:, fp * QB + tt * 128:fp * QB + (tt + 1) * 128],
                                    WQ2B[:, 4 * D + fp * D + fb * 512:
                                         4 * D + fp * D + (fb + 1) * 512],
                                    start=(fp == 0), stop=(fp == FPG - 1)), waits)
                            gtt = qb * 4 + tt
                            pe = tr.add("vector", lambda e, gtt=gtt, fb=fb:
                                        e.scalar_tensor_tensor(
                                X[:, gtt * D + fb * 512:gtt * D + (fb + 1) * 512],
                                PD[:, 0:512], ONEF[:, 0:1],
                                X[:, gtt * D + fb * 512:gtt * D + (fb + 1) * 512],
                                ALU.mult, ALU.add),
                                [E(pm)], sdeps=[preadd[gtt]])
                            at_state["pe_prev"] = pe
                            last_proj.append(pe)
            at_state["xread_guard"] = last_proj[-1]
            return last_proj

        xread_sdep = None
        for b in range(nb):
            xw = [] if b == 0 else [("d", sem_y, sem_y.count * 16, None)]
            for tt in range(NT):
                tr.dma("sync", lambda e, b=b, tt=tt: e.dma_start(
                    X[:, tt * D:(tt + 1) * D],
                    x_in.rearrange("(n p) c -> n p c", p=128)[b * NT + tt]),
                    sem_x, xw if tt == 0 else [])
            x_ready = [("d", sem_x, sem_x.count * 16, None)]
            arena_free = list(prep_done) if b == 0 else []
            if "f1" in phases:
                evs = ffn(x_ready, arena_free, xread_sdep)
                x_ready = [E(o) for o in evs[-8:]]
                xread_sdep = evs[-1]
                arena_free = []
            if "at" in phases:
                evs = attn(x_ready, arena_free, xread_sdep)
                x_ready = [E(o) for o in evs[-8:]]
                xread_sdep = evs[-1]
                arena_free = []
            if "f2" in phases:
                evs = ffn(x_ready, arena_free, xread_sdep)
                x_ready = [E(o) for o in evs[-8:]]
                xread_sdep = evs[-1]
            for tt in range(NT):
                tr.dma("sync", lambda e, b=b, tt=tt: e.dma_start(
                    y_out.rearrange("(n p) c -> n p c", p=128)[b * NT + tt],
                    X[:, tt * D:(tt + 1) * D]), sem_y, x_ready if tt == 0 else [])

        tr.add("sync", lambda e: e.nop(), [("d", sem_y, sem_y.count * 16, None)])
        tr.emit(nc, ctx)
    nc._dbg_ndrains = tr.n_drains
    return nc


_cache = {}


def _get_program(phases=("f1", "at", "f2")):
    key = tuple(phases)
    if key not in _cache:
        _cache[key] = build_program(phases)
    return _cache[key]


def _host_layouts(L0_attn_w_qkv, L0_attn_b_qkv, L0_attn_w_out, L0_attn_b_out,
                  L0_ffn_w1, L0_ffn_conv_w, L0_ffn_conv_b, L0_ffn_w2):
    wq = np.ascontiguousarray(np.asarray(L0_attn_w_qkv, np.float32))
    bq = np.asarray(L0_attn_b_qkv, np.float32).reshape(3 * D)
    bqcol = np.ascontiguousarray(bq[:3072].reshape(24, 128).T)
    bqv_row = np.ascontiguousarray(bq[2 * D:3 * D].reshape(1, D))
    bout_row = np.ascontiguousarray(np.asarray(L0_attn_b_out, np.float32).reshape(1, D))
    wo = np.ascontiguousarray(np.asarray(L0_attn_w_out, np.float32))
    w1 = np.ascontiguousarray(np.asarray(L0_ffn_w1, np.float32))
    cw = np.asarray(L0_ffn_conv_w, np.float32).reshape(HID, 3)
    cw_pt = np.ascontiguousarray(cw.reshape(NCT, 128, 3).transpose(1, 0, 2)
                                 .reshape(128, NCT * 3))
    cb = np.asarray(L0_ffn_conv_b, np.float32).reshape(HID)
    cb_pt = np.ascontiguousarray(cb.reshape(NCT, 128).T)
    w2 = np.ascontiguousarray(np.asarray(L0_ffn_w2, np.float32))
    return dict(w_qkv=wq, bqcol=bqcol, bqv_row=bqv_row, bout_row=bout_row,
                w_out=wo, w1=w1, cw_pt=cw_pt, cb_pt=cb_pt, w2=w2)


def kernel(L0_attn_x, L0_attn_w_qkv, L0_attn_b_qkv, L0_attn_w_out, L0_attn_b_out,
           L0_ffn_w1, L0_ffn_conv_w, L0_ffn_conv_b, L0_ffn_w2):
    x = np.ascontiguousarray(np.asarray(L0_attn_x, dtype=np.float32)).reshape(16 * S, D)
    shards = np.split(x, NCORES, axis=0)
    wmap = _host_layouts(L0_attn_w_qkv, L0_attn_b_qkv, L0_attn_w_out, L0_attn_b_out,
                         L0_ffn_w1, L0_ffn_conv_w, L0_ffn_conv_b, L0_ffn_w2)
    nc = _get_program()
    in_maps = [dict(x=s, **wmap) for s in shards]
    res = run_bass_kernel_spmd(nc, in_maps, core_ids=list(range(NCORES)))
    y = np.concatenate([r["y"] for r in res.results], axis=0)
    return y.reshape(16, S, D)


# version 03
# speedup vs baseline: 1.0547x; 1.0547x over previous
import numpy as np
import ml_dtypes
import concourse.bass as bass
import concourse.mybir as mybir
from concourse.bass_utils import run_bass_kernel_spmd
from contextlib import ExitStack

F32 = mybir.dt.float32
BF16 = mybir.dt.bfloat16
AF = mybir.ActivationFunctionType
ALU = mybir.AluOpType
AX = mybir.AxisListType

RC = 1.5 * 2 ** 23
EPS = 1e-5
NCORES = 8
NB = 2
S = 2048
D = 1024
HID = 4096
NT = S // 128
NDT = D // 128
NCT = HID // 128
WIN = 256
NWIN = S // WIN
QB = 512
NQB = S // QB
NKT = S // 128
NHG = 2
FPG = 4


class _Op:
    __slots__ = ("eng", "fn", "waits", "flag", "dma_sem", "post", "val",
                 "pos", "know", "pre_drain")

    def __init__(self, eng, fn, waits):
        self.eng = eng
        self.fn = fn
        self.waits = [w for w in waits if w is not None]
        self.flag = False
        self.dma_sem = None
        self.post = 0
        self.val = 0
        self.pos = 0
        self.know = {}
        self.pre_drain = False


class _Sem:
    __slots__ = ("name", "handle", "count")

    def __init__(self, name):
        self.name = name
        self.handle = None
        self.count = 0


class Trace:
    ENGS = ("sync", "gpsimd", "tensor", "vector", "scalar")

    def __init__(self):
        self.ops = []
        self.esem = {e: _Sem("es_" + e) for e in self.ENGS}
        self.dsems = {}
        self.eng_pos = {e: 0 for e in self.ENGS}
        self.eng_know = {e: {} for e in self.ENGS}
        self.n_drains = 0

    def dma_sem(self, name):
        if name not in self.dsems:
            self.dsems[name] = _Sem("ds_" + name)
        return self.dsems[name]

    def add(self, eng, fn, waits=(), sdeps=()):
        op = _Op(eng, fn, waits)
        self.eng_pos[eng] += 1
        op.pos = self.eng_pos[eng]
        know = dict(self.eng_know[eng])
        sdeps = list(sdeps)

        def join(prod):
            for k, v in prod.know.items():
                if know.get(k, -1) < v:
                    know[k] = v
            if know.get(prod.eng, -1) < prod.pos:
                know[prod.eng] = prod.pos

        for w in op.waits:
            if w[0] == "e":
                if w[1].eng == eng:
                    sdeps.append(w[1])
                else:
                    join(w[1])
            elif w[0] == "d" and len(w) > 3 and w[3] is not None:
                join(w[3])
        need = any(d is not None and d.pos > know.get(eng, 0) for d in sdeps)
        if need:
            op.pre_drain = True
            self.n_drains += 1
            know[eng] = op.pos - 1
        op.know = know
        self.eng_know[eng] = know
        self.ops.append(op)
        return op

    def dma(self, eng, fn, sem, waits=(), sdeps=()):
        w = [x for x in waits if x is not None]
        if sem.count > 0:
            w.append(("d", sem, sem.count * 16, None))
        op = self.add(eng, fn, w, sdeps)
        sem.count += 1
        op.dma_sem = sem
        op.post = sem.count * 16
        return op

    @staticmethod
    def ev(op):
        if op is None:
            return None
        if op.dma_sem is not None:
            return ("d", op.dma_sem, op.post, op)
        op.flag = True
        return ("e", op)

    def emit(self, nc, ctx):
        cnt = {e: 0 for e in self.esem}
        for op in self.ops:
            if op.dma_sem is None and op.flag:
                cnt[op.eng] += 1
                op.val = cnt[op.eng]
        for s in list(self.esem.values()) + list(self.dsems.values()):
            s.handle = ctx.enter_context(nc.semaphore(s.name))
        block = ctx.enter_context(nc.Block())

        def make_body(ename):
            my = [op for op in self.ops if op.eng == ename]

            def body(e):
                waited = {}
                for op in my:
                    if op.pre_drain:
                        e.drain()
                    for w in op.waits:
                        if w[0] == "d":
                            sem, val = w[1], w[2]
                        else:
                            if w[1].eng == ename:
                                continue
                            sem, val = self.esem[w[1].eng], w[1].val
                        if waited.get(id(sem), -1) < val:
                            waited[id(sem)] = val
                            e.wait_ge(sem.handle, val)
                    ins = op.fn(e)
                    if op.dma_sem is not None:
                        ins.then_inc(op.dma_sem.handle, 16)
                    elif op.flag:
                        ins.then_inc(self.esem[op.eng].handle, 1)
            return body

        for ename in self.ENGS:
            if any(op.eng == ename for op in self.ops):
                getattr(block, ename)(make_body(ename))


E = Trace.ev


def build_program(phases=("f1", "at", "f2"), nb=NB):
    nc = bass.Bass("TRN2", target_bir_lowering=False, debug=False)

    x_in = nc.dram_tensor("x", [nb * S, D], F32, kind="ExternalInput").ap()
    w_qkv = nc.dram_tensor("w_qkv", [D, 3 * D], F32, kind="ExternalInput").ap()
    bqcol = nc.dram_tensor("bqcol", [128, 24], F32, kind="ExternalInput").ap()
    bqv_row = nc.dram_tensor("bqv_row", [1, D], F32, kind="ExternalInput").ap()
    bout_row = nc.dram_tensor("bout_row", [1, D], F32, kind="ExternalInput").ap()
    w_out = nc.dram_tensor("w_out", [D, D], F32, kind="ExternalInput").ap()
    w1 = nc.dram_tensor("w1", [HID, D], F32, kind="ExternalInput").ap()
    cw_in = nc.dram_tensor("cw_pt", [128, 3 * NCT], F32, kind="ExternalInput").ap()
    cb_in = nc.dram_tensor("cb_pt", [128, NCT], F32, kind="ExternalInput").ap()
    w2 = nc.dram_tensor("w2", [D, HID], F32, kind="ExternalInput").ap()
    y_out = nc.dram_tensor("y", [nb * S, D], F32, kind="ExternalOutput").ap()

    wq1t = nc.dram_tensor("wq1t_scr", [NCT, 128, D], BF16, kind="Internal").ap()
    wq2t = nc.dram_tensor("wq2t_scr", [NCT, 128, D], BF16, kind="Internal").ap()
    ident = nc.inline_tensor(np.eye(128, dtype=np.float32), name="identf32").ap()

    tr = Trace()

    with ExitStack() as ctx:
        def sb(name, shape, dtype):
            return ctx.enter_context(nc.sbuf_tensor(name, shape, dtype))

        X = sb("X", [128, NT * D], F32)
        CW = sb("CW", [128, NCT * 3], F32)
        CB = sb("CB", [128, NCT], F32)
        BQCOL = sb("BQCOL", [128, 24], F32)
        IDF = sb("IDF", [128, 128], F32)
        ONEF = sb("ONEF", [128, 1], F32)
        NEGONE = sb("NEGONE", [128, 1], F32)
        EPSC = sb("EPSC", [128, 1], F32)
        ONEFR = sb("ONEFR", [1, 128], F32)
        ONEBR = sb("ONEBR", [1, 128], BF16)
        ONEBC = sb("ONEBC", [128, 1], BF16)
        RCCOL = sb("RCCOL", [128, 1], F32)
        NRCCOL = sb("NRCCOL", [128, 1], F32)
        WSC = sb("WSC", [128, 4], F32)
        BQVB = sb("BQVB", [128, D], F32)
        BOUTB = sb("BOUTB", [128, D], F32)
        STATS = sb("STATS", [128, 64], F32)
        AMAX1 = sb("AMAX1", [128, NT], F32)
        S1C = sb("S1C", [128, NT], F32)
        F1C = sb("F1C", [128, NT], F32)
        AMAX2 = sb("AMAX2", [128, NT], F32)
        S2CC = sb("S2CC", [128, NT], F32)
        FAC2 = sb("FAC2", [128, NT], F32)
        SCR16 = sb("SCR16", [128, NT], F32)
        F1ROW = sb("F1ROW", [1, S], F32)
        S2ROW = sb("S2ROW", [1, WIN], F32)
        ROWS = sb("ROWS", [1, 64], F32)

        AR_A = sb("AR_A", [128, NDT * S], BF16)
        AR_B = sb("AR_B", [128, NCT * WIN], F32)
        AR_C = sb("AR_C", [128, NCT * WIN], BF16)
        WQ1B = sb("WQ1B", [128, 4 * D], BF16)
        WQ2B = sb("WQ2B", [128, 8 * D], BF16)
        HS = sb("HS", [128, 2 * (WIN + 2)], F32)
        TA = sb("TA", [128, 2 * WIN], F32)
        TB = sb("TB", [128, 2 * WIN], F32)
        F1B = sb("F1B", [128, WIN + 2], F32)
        S2B = sb("S2B", [128, WIN], F32)
        ACC = sb("ACC", [128, WIN], F32)
        QR = sb("QR", [128, 2 * WIN], F32)
        BST = sb("BST", [128, 12], F32)
        AVT = sb("AVT", [128, FPG * QB], BF16)
        NEGB = sb("NEGB", [128, 8], F32)
        BCS = TA.bitcast(F32)[0:64, 0:QB]
        AVST = HS.bitcast(BF16)[0:64, 0:QB]
        QSQ = F1B.bitcast(BF16)[:, 0:QB]
        ROWR = TB[0:1, 0:QB]
        ROWB = ACC.bitcast(BF16)[0:1, 0:QB]

        LN1 = AR_C.bitcast(F32)
        QRL = AR_C.bitcast(F32)
        XQ = AR_B.bitcast(BF16)
        H2 = AR_B
        XQT = AR_A
        XQ2 = AR_C
        L2T = AR_A
        L2TOK = AR_B.bitcast(BF16)
        ARBB = AR_B.bitcast(BF16)
        KT = AR_C
        VBLK = 8 * 65
        E_OFF = NKT * VBLK
        QT_OFF = E_OFF + 4096

        PS1 = ctx.enter_context(nc.psum_tensor("PS1", [128, 1024], F32))
        PS2 = ctx.enter_context(nc.psum_tensor("PS2", [128, 1024], F32))
        PA = ctx.enter_context(nc.psum_tensor("PA", [128, 512], F32))
        PB = ctx.enter_context(nc.psum_tensor("PB", [128, 512], F32))
        PC = ctx.enter_context(nc.psum_tensor("PC", [128, 512], F32))
        PD = ctx.enter_context(nc.psum_tensor("PD", [128, 512], F32))

        sem_x = tr.dma_sem("xld")
        sem_y = tr.dma_sem("yst")
        sem_m1 = [tr.dma_sem("wq1_%d" % i) for i in range(3)]
        sem_m2 = [tr.dma_sem("wq2_%d" % i) for i in range(3)]
        sem_tp = [tr.dma_sem("tp_%d" % i) for i in range(2)]
        sem_wl = [tr.dma_sem("wl_%d" % i) for i in range(4)]
        sem_sw = [tr.dma_sem("sw_%d" % i) for i in range(2)]
        sem_qk = [tr.dma_sem("qk_%d" % i) for i in range(2)]
        sem_sh = tr.dma_sem("shift")

        tr.add("vector", lambda e: e.memset(ONEF[:], 1.0))
        tr.add("vector", lambda e: e.memset(NEGONE[:], -1.0))
        tr.add("vector", lambda e: e.memset(EPSC[:], EPS))
        tr.add("vector", lambda e: e.memset(ONEFR[:], 1.0))
        tr.add("vector", lambda e: e.memset(ONEBR[:], 1.0))
        tr.add("vector", lambda e: e.memset(ONEBC[:], 1.0))
        tr.add("vector", lambda e: e.memset(RCCOL[:], RC))
        op_consts = tr.add("vector", lambda e: e.memset(NRCCOL[:], -RC))

        ld_id = tr.dma("sync", lambda e: e.dma_start(IDF[:], ident), sem_wl[0])
        ld_cw = tr.dma("sync", lambda e: e.dma_start(CW[:], cw_in), sem_wl[0])
        ld_cb = tr.dma("sync", lambda e: e.dma_start(CB[:], cb_in), sem_wl[0])
        ld_bq = tr.dma("sync", lambda e: e.dma_start(BQCOL[:], bqcol), sem_wl[0])
        ld_bv = tr.dma("sync", lambda e: e.dma_start(F1ROW[0:1, 0:D], bqv_row), sem_wl[0])
        ld_bo = tr.dma("sync", lambda e: e.dma_start(F1ROW[0:1, D:2 * D], bout_row), sem_wl[0])

        bc_prev = None
        for (dst, base) in ((BQVB, 0), (BOUTB, D)):
            for c0 in (0, 512):
                m = tr.add("tensor", lambda e, base=base, c0=c0: e.matmul(
                    PC[:, 0:512], ONEFR[0:1, :], F1ROW[0:1, base + c0:base + c0 + 512],
                    start=True, stop=True),
                    [E(ld_bv), E(ld_bo), E(op_consts), E(bc_prev)])
                bc_prev = tr.add("scalar", lambda e, dst=dst, c0=c0: e.copy(
                    dst[:, c0:c0 + 512], PC[:, 0:512]), [E(m)],
                    sdeps=[bc_prev])
        op_bias_b = bc_prev

        W1LD = AR_B
        WQBF = AR_C
        STG = AR_A
        PSUMW = STATS

        def prep_matrix(src_ap, nblk, blk_cols, nslot, ws_col, inv_col, inv_scale,
                        scratch, is_w1, rows_c0, prev_events):
            reds = []
            for i in range(nblk):
                w = list(prev_events) if i < nslot else [E(reds[i - nslot])]
                q = "sync" if i % 2 == 0 else "gpsimd"
                ld = tr.dma(q, lambda e, i=i, nslot=nslot: e.dma_start(
                    W1LD[:, (i % nslot) * blk_cols:(i % nslot + 1) * blk_cols],
                    src_ap.rearrange("(n p) c -> n p c", p=128)[i]),
                    sem_wl[i % len(sem_wl)], w)
                red = tr.add("vector", lambda e, i=i, nslot=nslot: e.tensor_reduce(
                    PSUMW[:, i:i + 1],
                    W1LD[:, (i % nslot) * blk_cols:(i % nslot + 1) * blk_cols],
                    AX.X, ALU.add, apply_absolute_value=True), [E(ld)])
                reds.append(red)
            redall = tr.add("vector", lambda e: e.tensor_reduce(
                PSUMW[:, 40:41], PSUMW[:, 0:nblk], AX.X, ALU.add), [],
                sdeps=[reds[-1]])
            msum = tr.add("tensor", lambda e: e.matmul(
                PC[0:1, 0:1], PSUMW[:, 40:41], ONEF[:, 0:1], start=True, stop=True),
                [E(redall)] + list(prev_events))
            c0 = rows_c0
            t1 = tr.add("scalar", lambda e: e.copy(ROWS[0:1, c0:c0 + 1], PC[0:1, 0:1]),
                        [E(msum)])
            t2 = tr.add("vector", lambda e: e.tensor_scalar(
                ROWS[0:1, c0 + 1:c0 + 2], ROWS[0:1, c0:c0 + 1], 1.0 / float(HID * D),
                1e-5, ALU.mult, ALU.max), [E(t1)])
            t3 = tr.add("vector", lambda e: e.reciprocal(
                ROWS[0:1, c0 + 2:c0 + 3], ROWS[0:1, c0 + 1:c0 + 2]), [], sdeps=[t2])
            t4 = tr.add("vector", lambda e: e.reciprocal(
                ROWS[0:1, c0 + 3:c0 + 4], ROWS[0:1, c0 + 2:c0 + 3]), [], sdeps=[t3])
            t4b = tr.add("vector", lambda e: e.tensor_scalar(
                ROWS[0:1, c0 + 4:c0 + 5], ROWS[0:1, c0 + 3:c0 + 4], inv_scale, None,
                ALU.mult), [], sdeps=[t4])
            mws = tr.add("tensor", lambda e: e.matmul(
                PC[:, 0:1], ONEFR[0:1, :], ROWS[0:1, c0 + 2:c0 + 3], start=True,
                stop=True), [E(t4b), E(t1)])
            cws = tr.add("scalar", lambda e: e.copy(
                WSC[:, ws_col:ws_col + 1], PC[:, 0:1]), [E(mws)], sdeps=[t1])
            minv = tr.add("tensor", lambda e: e.matmul(
                PC[:, 1:2], ONEFR[0:1, :], ROWS[0:1, c0 + 4:c0 + 5], start=True,
                stop=True), [E(cws)])
            cinv = tr.add("scalar", lambda e: e.copy(
                WSC[:, inv_col:inv_col + 1], PC[:, 1:2]), [E(minv)], sdeps=[cws])
            prev_q = {}
            stores = []
            tps = []
            for i in range(nblk):
                wslot = (i % nslot) * blk_cols
                q = "sync" if i % 2 == 0 else "gpsimd"
                ld = tr.dma(q, lambda e, wslot=wslot, i=i: e.dma_start(
                    W1LD[:, wslot:wslot + blk_cols],
                    src_ap.rearrange("(n p) c -> n p c", p=128)[i]),
                    sem_wl[i % len(sem_wl)],
                    [E(prev_q.get(i % nslot, redall))])
                q1 = tr.add("vector", lambda e, wslot=wslot: e.tensor_scalar(
                    W1LD[:, wslot:wslot + blk_cols], W1LD[:, wslot:wslot + blk_cols],
                    WSC[:, ws_col:ws_col + 1], RC, ALU.mult, ALU.add),
                    [E(ld), E(cinv)] + (list(prev_events) if i == 0 else []))
                q2 = tr.add("scalar", lambda e, wslot=wslot: e.activation(
                    W1LD[:, wslot:wslot + blk_cols], W1LD[:, wslot:wslot + blk_cols],
                    AF.Identity, bias=NRCCOL[:, 0:1]), [E(q1)])
                q3 = tr.add("vector", lambda e, wslot=wslot: e.tensor_scalar(
                    WQBF[:, wslot:wslot + blk_cols], W1LD[:, wslot:wslot + blk_cols],
                    -1.0, 1.0, ALU.max, ALU.min),
                    [E(q2)] + (list(prev_events) if i == 0 else [])
                    + ([E(tps[i - nslot])] if i >= nslot else []))
                prev_q[i % nslot] = q3
                ntile = blk_cols // 128
                tp = tr.dma("scalar", lambda e, wslot=wslot, ntile=ntile:
                            e.dma_start_transpose(
                    STG[:, 0:blk_cols].rearrange("p (n f) -> p n f", n=ntile),
                    WQBF[:, wslot:wslot + blk_cols].rearrange("p (n f) -> p n f",
                                                              n=ntile)),
                    sem_tp[i % 2],
                    [E(q3)] + ([E(stores[-1])] if stores else list(prev_events)))
                tps.append(tp)
                if is_w1:
                    stx = tr.dma("sync", lambda e, i=i: e.dma_start(
                        scratch[i], STG[:, 0:blk_cols]), sem_sw[0], [E(tp)])
                else:
                    stx = tr.dma("sync", lambda e, i=i: e.dma_start(
                        scratch[:, :, i * 128:(i + 1) * 128]
                        .rearrange("ct p fm -> p ct fm"),
                        STG[:, 0:blk_cols].rearrange("p (ct fm) -> p ct fm", fm=128)),
                        sem_sw[1], [E(tp)])
                stores.append(stx)
            return stores, q3, cinv

        w1_stores, w1_lastq, w1_cinv = prep_matrix(
            w1, NCT, D, 4, 2, 0, 1.0, wq1t, True, 8, [E(ld_id), E(op_bias_b)])
        w2_stores, w2_lastq, w2_cinv = prep_matrix(
            w2, NDT, HID, 2, 3, 1, 0.5, wq2t, False, 16,
            [E(w1_lastq), E(w1_stores[-1])])
        w1_done = ("d", tr.dsems["sw_0"], tr.dsems["sw_0"].count * 16, w1_stores[-1])
        w2_done = ("d", tr.dsems["sw_1"], tr.dsems["sw_1"].count * 16, w2_stores[-1])
        prep_done = [w1_done, w2_done, E(w2_lastq)]

        def layernorm_tile(tt, out_ap, waits, sdeps=()):
            s1 = tr.add("vector", lambda e, tt=tt: e.bn_stats(
                BST[:, 0:6], X[:, tt * D:tt * D + 512]), waits, sdeps=sdeps)
            tr.add("vector", lambda e, tt=tt: e.bn_stats(
                BST[:, 6:12], X[:, tt * D + 512:tt * D + 1024]), [])
            ag = tr.add("vector", lambda e, tt=tt: e.bn_aggr(
                STATS[:, 2 * tt:2 * tt + 2], BST[:]), [], sdeps=[s1])
            t1 = tr.add("scalar", lambda e, tt=tt: e.activation(
                STATS[:, 32 + tt:33 + tt], STATS[:, 2 * tt + 1:2 * tt + 2],
                AF.Identity, bias=EPSC[:, 0:1]), [E(ag)])
            t2 = tr.add("vector", lambda e, tt=tt: e.reciprocal(
                STATS[:, 48 + tt:49 + tt], STATS[:, 32 + tt:33 + tt]), [E(t1)],
                sdeps=[ag])
            t3 = tr.add("scalar", lambda e, tt=tt: e.activation(
                STATS[:, 32 + tt:33 + tt], STATS[:, 48 + tt:49 + tt], AF.Sqrt),
                [E(t2)], sdeps=[t1])
            t4 = tr.add("vector", lambda e, tt=tt: e.scalar_tensor_tensor(
                STATS[:, 48 + tt:49 + tt], STATS[:, 2 * tt:2 * tt + 1],
                NEGONE[:, 0:1], STATS[:, 32 + tt:33 + tt], ALU.mult, ALU.mult),
                [E(t3)], sdeps=[t2])
            ap_op = tr.add("scalar", lambda e, tt=tt, out_ap=out_ap: e.activation(
                out_ap, X[:, tt * D:(tt + 1) * D], AF.Identity,
                bias=STATS[:, 48 + tt:49 + tt], scale=STATS[:, 32 + tt:33 + tt]),
                [E(t4)], sdeps=[t3])
            return ap_op

        ffn_state = {"m1dma": {}, "m2dma": {}, "m1cur": None, "m2cur": None}

        def ffn(x_ready, arena_free, xread_sdep):
            pass

            round_ops = []
            prev_r1 = None
            for tt in range(NT):
                lnslot = (tt % 2) * D
                lnw = list(x_ready) + list(arena_free) if tt == 0 else []
                ap_op = layernorm_tile(
                    tt, LN1[:, lnslot:lnslot + D], lnw,
                    sdeps=([round_ops[-1]] if tt >= 2 else
                           ([xread_sdep] if tt == 0 else [])))
                am = tr.add("vector", lambda e, tt=tt, lnslot=lnslot: e.tensor_reduce(
                    AMAX1[:, tt:tt + 1], LN1[:, lnslot:lnslot + D], AX.X, ALU.max,
                    apply_absolute_value=True), [E(ap_op)])
                c1 = tr.add("vector", lambda e, tt=tt: e.tensor_scalar(
                    SCR16[:, tt:tt + 1], AMAX1[:, tt:tt + 1], 1e-5, None, ALU.max),
                    [], sdeps=[am])
                c2 = tr.add("vector", lambda e, tt=tt: e.reciprocal(
                    S1C[:, tt:tt + 1], SCR16[:, tt:tt + 1]), [], sdeps=[c1])
                c3 = tr.add("vector", lambda e, tt=tt: e.tensor_scalar(
                    S1C[:, tt:tt + 1], S1C[:, tt:tt + 1], 127.0, None, ALU.mult),
                    [], sdeps=[c2])
                qrs = 2048 + (tt % 2) * D
                r1 = tr.add("scalar", lambda e, tt=tt, lnslot=lnslot, qrs=qrs:
                            e.activation(
                    QRL[:, qrs:qrs + D], LN1[:, lnslot:lnslot + D], AF.Identity,
                    bias=RCCOL[:, 0:1], scale=S1C[:, tt:tt + 1]),
                    [E(c3), E(op_consts)], sdeps=[prev_r1])
                prev_r1 = r1
                r2 = tr.add("vector", lambda e, tt=tt, qrs=qrs: e.tensor_scalar(
                    XQ[:].rearrange("p (dt n c) -> p dt n c", dt=NDT, n=NT)[:, :, tt, :],
                    QRL[:, qrs:qrs + D].rearrange("p (dt c) -> p dt c", c=128),
                    -RC, None, ALU.add),
                    [E(r1)] + (list(arena_free) if tt == 0 else []))
                round_ops.append(r2)
            f1a = tr.add("vector", lambda e: e.reciprocal(F1C[:], S1C[:]), [],
                         sdeps=[round_ops[-1]])
            f1b_op = tr.add("vector", lambda e: e.tensor_scalar(
                F1C[:], F1C[:], WSC[:, 0:1], None, ALU.mult),
                [w1_done, E(w1_cinv)], sdeps=[f1a])
            f1row_ops = []
            prev_cp = None
            for tt in range(NT):
                tp = tr.add("tensor", lambda e, tt=tt: e.transpose(
                    PD[0:1, 0:128], F1C[:, tt:tt + 1], IDF[:]),
                    [E(f1b_op), E(ld_id), E(prev_cp)])
                prev_cp = tr.add("scalar", lambda e, tt=tt: e.copy(
                    F1ROW[0:1, tt * 128:(tt + 1) * 128], PD[0:1, 0:128]), [E(tp)])
                f1row_ops.append(prev_cp)
            tp_ops = []
            for dt in range(NDT):
                tp_ops.append(tr.dma("scalar", lambda e, dt=dt: e.dma_start_transpose(
                    XQT[:, dt * S:(dt + 1) * S].rearrange("p (n f) -> p n f", n=NT),
                    XQ[:, dt * S:(dt + 1) * S].rearrange("p (n f) -> p n f", n=NT)),
                    sem_tp[dt % 2], [E(round_ops[-1])] + list(arena_free)))
            xqt_done = [E(t) for t in tp_ops]

            st = {"evac": {}, "silu_last": {}, "ra_last": {}, "round_last": {},
                  "m2stop": {}, "evr": {}, "sbc": None, "fcp": None,
                  "red_prev": None, "s2rowcp": None, "g_last": None,
                  "m2clear": None}
            all_evr = []

            def emit_m1_header(w):
                cin0 = max(WIN * w - 1, 0)
                wid = min(WIN * w + WIN + 1, S) - cin0
                fm = tr.add("tensor", lambda e, cin0=cin0, wid=wid: e.matmul(
                    PC[:, 0:wid], ONEFR[0:1, :], F1ROW[0:1, cin0:cin0 + wid],
                    start=True, stop=True),
                    [E(f1row_ops[-1]), E(st["sbc"])])
                st["fcp"] = tr.add("scalar", lambda e, wid=wid: e.copy(
                    F1B[:, 0:wid], PC[:, 0:wid]), [E(fm)],
                    sdeps=[st["s2rowcp"]])

            def emit_m1_ct(w, ct):
                cin0 = max(WIN * w - 1, 0)
                wid = min(WIN * w + WIN + 1, S) - cin0
                off = WIN * w - cin0
                jl0 = 1 if w == 0 else 0
                jr1 = WIN - 1 if w == NWIN - 1 else WIN
                g = ct // 2
                if ct % 2 == 0:
                    bufsel = g % 2
                    dw = [ffn_state["m1dma"].get(bufsel)]
                    if w == 0 and ct == 0:
                        dw += [w1_done] + list(x_ready)
                    ffn_state["m1cur"] = tr.dma(
                        "sync", lambda e, g=g, bufsel=bufsel: e.dma_start(
                            WQ1B[:, bufsel * 2 * D:(bufsel + 1) * 2 * D]
                            .rearrange("p (ct c) -> p ct c", c=D),
                            wq1t[2 * g:2 * g + 2].rearrange("ct p c -> p ct c")),
                        sem_m1[g % 3], dw)
                cur = ffn_state["m1cur"]
                lct = ct % 2
                bufsel = g % 2
                h1ps = [PS1[:, 0:512], PS1[:, 512:1024],
                        PS2[:, 0:512], PS2[:, 512:1024]][ct % 4]
                pslot = ct % 4
                mm = None
                for dt in range(NDT):
                    waits = []
                    if dt == 0:
                        waits = [("d", cur.dma_sem, cur.post, cur),
                                 E(st["evac"].get(pslot))]
                        if ct % 4 >= 2:
                            waits.append(E(st["m2clear"]))
                        if w == 0 and ct <= 1:
                            waits = waits + xqt_done
                    mm = tr.add("tensor", lambda e, dt=dt, cin0=cin0, wid=wid,
                                h1ps=h1ps, bufsel=bufsel, lct=lct: e.matmul(
                        h1ps[:, 0:wid],
                        WQ1B[:, bufsel * 2 * D + lct * D + dt * 128:
                             bufsel * 2 * D + lct * D + (dt + 1) * 128],
                        XQT[:, dt * S + cin0:dt * S + cin0 + wid],
                        start=(dt == 0), stop=(dt == NDT - 1)), waits)
                if ct % 2 == 1:
                    ffn_state["m1dma"][bufsel] = E(mm)
                hslot = (ct % 2) * (WIN + 2)
                ev = tr.add("vector", lambda e, wid=wid, h1ps=h1ps, hslot=hslot:
                            e.tensor_mul(HS[:, hslot:hslot + wid],
                                         h1ps[:, 0:wid], F1B[:, 0:wid]),
                            [E(mm), E(st["fcp"]),
                             E(st["silu_last"].get(w - 1 if ct <= 1 else w))])
                st["evac"][pslot] = ev
                taslot = (ct % 2) * WIN
                cwc = ct * 3
                tR = tr.add("scalar", lambda e, hslot=hslot, taslot=taslot,
                            jr1=jr1, off=off, cwc=cwc: e.activation(
                    TA[:, taslot:taslot + jr1],
                    HS[:, hslot + off + 1:hslot + off + 1 + jr1],
                    AF.Identity, scale=CW[:, cwc + 2:cwc + 3]),
                    [E(ev), E(ld_cw)],
                    sdeps=[st["silu_last"].get(w - 1 if ct <= 1 else w)])
                after_tr = [tR]
                if jr1 < WIN:
                    z = tr.add("vector", lambda e, taslot=taslot, jr1=jr1: e.memset(
                        TA[:, taslot + jr1:taslot + WIN], 0.0), [E(tR)])
                    after_tr.append(z)
                tL = tr.add("vector", lambda e, hslot=hslot, taslot=taslot,
                            jl0=jl0, off=off, cwc=cwc: e.scalar_tensor_tensor(
                    TB[:, taslot + jl0:taslot + WIN],
                    HS[:, hslot + off + jl0 - 1:hslot + off + WIN - 1],
                    CW[:, cwc:cwc + 1],
                    TA[:, taslot + jl0:taslot + WIN], ALU.mult, ALU.add),
                    [E(o) for o in after_tr])
                if jl0 > 0:
                    tr.add("vector", lambda e, taslot=taslot: e.tensor_copy(
                        TB[:, taslot:taslot + 1], TA[:, taslot:taslot + 1]),
                        [E(tR)])
                tM = tr.add("vector", lambda e, hslot=hslot, taslot=taslot,
                            off=off, cwc=cwc: e.scalar_tensor_tensor(
                    TB[:, taslot:taslot + WIN],
                    HS[:, hslot + off:hslot + off + WIN],
                    CW[:, cwc + 1:cwc + 2],
                    TB[:, taslot:taslot + WIN], ALU.mult, ALU.add), [],
                    sdeps=[tL])
                sl = tr.add("scalar", lambda e, ct=ct, taslot=taslot: e.activation(
                    H2[:, ct * WIN:(ct + 1) * WIN], TB[:, taslot:taslot + WIN],
                    AF.Silu, bias=CB[:, ct:ct + 1]),
                    [E(tM), E(ld_cb)],
                    sdeps=[st["ra_last"].get(w - 1), tR])
                st["silu_last"][w] = sl

            def emit_m2_ct(w, ct):
                psg = [PS2[:, 0:512], PS2[:, 512:1024], PA[:], PB[:]]
                g = ct // 2
                if ct % 2 == 0:
                    bufsel = g % 2
                    dw = [ffn_state["m2dma"].get(bufsel)]
                    if w == 0 and ct == 0:
                        dw += [w2_done] + list(x_ready)
                    ffn_state["m2cur"] = tr.dma(
                        "sync", lambda e, g=g, bufsel=bufsel: e.dma_start(
                            WQ2B[:, bufsel * 2 * D:(bufsel + 1) * 2 * D]
                            .rearrange("p (ct c) -> p ct c", c=D),
                            wq2t[2 * g:2 * g + 2].rearrange("ct p c -> p ct c")),
                        sem_m2[g % 3], dw)
                cur = ffn_state["m2cur"]
                lct = ct % 2
                bufsel = g % 2
                mm2 = None
                for tt in range(2):
                    for fb in range(2):
                        waits = []
                        if tt == 0 and fb == 0:
                            waits = [("d", cur.dma_sem, cur.post, cur)]
                            if ct == 0:
                                waits += [E(st["round_last"][w]),
                                          E(st["evac"].get(2)), E(st["evac"].get(3))]
                                waits += [E(o) for o in st["evr"].get(w - 2, [])]
                        mm2 = tr.add("tensor", lambda e, ct=ct, tt=tt, fb=fb,
                                     bufsel=bufsel, lct=lct: e.matmul(
                            psg[tt * 2 + fb],
                            XQ2[:, ct * WIN + tt * 128:ct * WIN + (tt + 1) * 128],
                            WQ2B[:, bufsel * 2 * D + lct * D + fb * 512:
                                 bufsel * 2 * D + lct * D + (fb + 1) * 512],
                            start=(ct == 0), stop=(ct == NCT - 1)), waits)
                if ct % 2 == 1:
                    ffn_state["m2dma"][bufsel] = E(mm2)
                if ct == NCT - 1:
                    st["m2stop"][w] = mm2

            def emit_m2_evacs(w):
                psg = [PS2[:, 0:512], PS2[:, 512:1024], PA[:], PB[:]]
                evrs = []
                prev = None
                for tt in range(2):
                    for fb in range(2):
                        gtt = 2 * w + tt
                        evr = tr.add("vector", lambda e, tt=tt, fb=fb, gtt=gtt:
                                     e.scalar_tensor_tensor(
                            X[:, gtt * D + fb * 512:gtt * D + (fb + 1) * 512],
                            psg[tt * 2 + fb], FAC2[:, gtt:gtt + 1],
                            X[:, gtt * D + fb * 512:gtt * D + (fb + 1) * 512],
                            ALU.mult, ALU.add),
                            [E(st["m2stop"][w]), E(st["fc2_%d" % w])],
                            sdeps=[st["fc2_%d" % w]] if prev is None else [])
                        prev = evr
                        evrs.append(evr)
                st["evr"][w] = evrs
                st["m2clear"] = evrs[1]
                all_evr.extend(evrs)

            def emit_tail(w):
                red = tr.add("vector", lambda e: e.tensor_reduce(
                    ACC[:], H2[:].rearrange("p (ct j) -> p j ct", j=WIN), AX.X,
                    ALU.max, apply_absolute_value=True),
                    [E(st["silu_last"][w])], sdeps=[st["red_prev"]])
                st["red_prev"] = red
                prev = red
                for i in range(2):
                    tp = tr.add("tensor", lambda e, i=i: e.transpose(
                        PD[:, 0:128], ACC[:, i * 128:(i + 1) * 128], IDF[:]),
                        [E(prev)])
                    rd = tr.add("vector", lambda e, i=i, w=w: e.tensor_reduce(
                        AMAX2[:, 2 * w + i:2 * w + i + 1], PD[:, 0:128], AX.X,
                        ALU.max), [E(tp)], sdeps=[red])
                    prev = rd
                s2a = tr.add("vector", lambda e, w=w: e.tensor_scalar(
                    SCR16[:, 0:2], AMAX2[:, 2 * w:2 * w + 2], 1e-5, None, ALU.max),
                    [], sdeps=[prev])
                s2b_ = tr.add("vector", lambda e, w=w: e.reciprocal(
                    S2CC[:, 2 * w:2 * w + 2], SCR16[:, 0:2]), [], sdeps=[s2a])
                s2c = tr.add("vector", lambda e, w=w: e.tensor_scalar(
                    S2CC[:, 2 * w:2 * w + 2], S2CC[:, 2 * w:2 * w + 2], 127.0, None,
                    ALU.mult), [], sdeps=[s2b_])
                fc1 = tr.add("vector", lambda e, w=w: e.reciprocal(
                    FAC2[:, 2 * w:2 * w + 2], S2CC[:, 2 * w:2 * w + 2]), [],
                    sdeps=[s2c])
                st["fc2_%d" % w] = tr.add("vector", lambda e, w=w: e.tensor_scalar(
                    FAC2[:, 2 * w:2 * w + 2], FAC2[:, 2 * w:2 * w + 2], WSC[:, 1:2],
                    None, ALU.mult), [w2_done, E(w2_cinv)], sdeps=[fc1])
                rowcp = st["s2rowcp"]
                for i in range(2):
                    tps2 = tr.add("tensor", lambda e, i=i, w=w: e.transpose(
                        PD[0:1, 0:128], S2CC[:, 2 * w + i:2 * w + i + 1], IDF[:]),
                        [E(s2c), E(rowcp)])
                    rowcp = tr.add("scalar", lambda e, i=i: e.copy(
                        S2ROW[0:1, i * 128:(i + 1) * 128], PD[0:1, 0:128]),
                        [E(tps2)], sdeps=[st["fcp"]] if i == 0 else [])
                st["s2rowcp"] = rowcp
                sbm = tr.add("tensor", lambda e: e.matmul(
                    PC[:, 0:WIN], ONEFR[0:1, :], S2ROW[0:1, 0:WIN], start=True,
                    stop=True), [E(rowcp), E(st["fcp"])])
                sbc = tr.add("scalar", lambda e: e.copy(S2B[:], PC[:, 0:WIN]),
                             [E(sbm)], sdeps=[rowcp])
                st["sbc"] = sbc
                ra = None
                rbs = []
                for ct in range(NCT):
                    qs = (ct % 2) * WIN
                    mu = tr.add("gpsimd", lambda e, ct=ct: e.tensor_mul(
                        H2[:, ct * WIN:(ct + 1) * WIN],
                        H2[:, ct * WIN:(ct + 1) * WIN], S2B[:]),
                        [E(sbc)] if ct == 0 else [],
                        sdeps=[st["g_last"]] if ct == 0 else [])
                    if ct == 0:
                        st["g_last"] = None
                    ra = tr.add("scalar", lambda e, ct=ct, qs=qs: e.activation(
                        QR[:, qs:qs + WIN], H2[:, ct * WIN:(ct + 1) * WIN],
                        AF.Identity, bias=RCCOL[:, 0:1]),
                        [E(mu)] + ([E(st["m2stop"].get(w - 1))] if ct == 0 else [])
                        + ([E(rbs[ct - 2])] if ct >= 2 else []),
                        sdeps=[st["silu_last"][w] if ct == 0 else None])
                    rb = tr.add("scalar", lambda e, ct=ct, qs=qs: e.activation(
                        XQ2[:, ct * WIN:(ct + 1) * WIN], QR[:, qs:qs + WIN],
                        AF.Identity, bias=NRCCOL[:, 0:1]), [E(ra)])
                    rbs.append(rb)
                st["ra_last"][w] = ra
                st["round_last"][w] = rbs[-1]

            for w in range(NWIN + 1):
                if w < NWIN:
                    emit_m1_header(w)
                    for ct in range(NCT):
                        emit_m1_ct(w, ct)
                if w >= 1:
                    for ct in range(NCT):
                        emit_m2_ct(w - 1, ct)
                    emit_m2_evacs(w - 1)
                if w < NWIN:
                    emit_tail(w)
            return all_evr

        at_state = {"exp_prev": None, "av_prev": None, "pair_evacs": None,
                    "pe_prev": None, "bc_prev": None, "shift_prev": None,
                    "sq_guard": None, "s2_guard": None, "mx_prev": None,
                    "rowb_guard": None, "xread_guard": None}

        def attn(x_ready, arena_free, xread_sdep):
            ln_last = None
            for tt in range(NT):
                ln_last = layernorm_tile(
                    tt, L2TOK[:].rearrange("p (dt n c) -> p dt n c", dt=NDT, n=NT)
                    [:, :, tt, :],
                    (list(x_ready) + list(arena_free)) if tt == 0 else [],
                    sdeps=[xread_sdep] if tt == 0 else [])
            ltp = []
            for dt in range(NDT):
                ltp.append(tr.dma("scalar", lambda e, dt=dt: e.dma_start_transpose(
                    L2T[:, dt * S:(dt + 1) * S].rearrange("p (n f) -> p n f", n=NT),
                    L2TOK[:, dt * S:(dt + 1) * S].rearrange("p (n f) -> p n f", n=NT)),
                    sem_tp[dt % 2], [E(ln_last)] + list(arena_free)))
            l2t_done = [E(t) for t in ltp]

            preadd = []
            pa_prev = None
            for tt in range(NT):
                pa_ = tr.add("vector", lambda e, tt=tt: e.tensor_add(
                    X[:, tt * D:(tt + 1) * D], X[:, tt * D:(tt + 1) * D], BOUTB[:]),
                    [E(ln_last), E(op_bias_b)] if tt == 0 else [],
                    sdeps=[xread_sdep] if tt == 0 else [])
                preadd.append(pa_)
                pa_prev = pa_

            last_proj = []
            for hg in range(NHG):
                guard = l2t_done if hg == 0 else [E(o) for o in last_proj[-8:]]
                kt_last_evac = None
                ksq_prev = None
                kw_consumed = None
                k2m_init = tr.add("vector", lambda e: e.memset(ROWS[0:1, 0:16], 0.0),
                                  guard,
                                  sdeps=[at_state["mx_prev"], pa_prev])
                for fp in range(FPG):
                    pair = hg * FPG + fp
                    kw = tr.dma("gpsimd", lambda e, pair=pair: e.dma_start(
                        WQ1B[:, 0:D].rearrange("p (dt c) -> p dt c", c=128),
                        w_qkv.rearrange("(dt p) c -> p dt c", p=128)
                        [:, :, D + pair * 128:D + (pair + 1) * 128]), sem_qk[0],
                        guard + [E(kw_consumed)])
                    km = None
                    for Nb in range(4):
                        for dt in range(NDT):
                            waits = []
                            if dt == 0:
                                waits = [("d", kw.dma_sem, kw.post, kw),
                                         E(kt_last_evac)]
                                if Nb == 0 and fp == 0:
                                    waits = waits + guard
                            km = tr.add("tensor", lambda e, fp=fp, Nb=Nb, dt=dt:
                                        e.matmul(
                                PC[:, 0:512], WQ1B[:, dt * 128:(dt + 1) * 128],
                                L2T[:, dt * S + Nb * 512:dt * S + (Nb + 1) * 512],
                                start=(dt == 0), stop=(dt == NDT - 1)), waits)
                        ke = tr.add("scalar", lambda e, fp=fp, Nb=Nb, pair=pair:
                                    e.activation(
                            KT[:, fp * S + Nb * 512:fp * S + (Nb + 1) * 512],
                            PC[:, 0:512], AF.Identity,
                            bias=BQCOL[:, 8 + pair:9 + pair]), [E(km), E(ld_bq)])
                        kt_last_evac = ke
                        sq = tr.add("vector", lambda e, fp=fp, Nb=Nb: e.tensor_mul(
                            QSQ[:], KT[:, fp * S + Nb * 512:fp * S + (Nb + 1) * 512],
                            KT[:, fp * S + Nb * 512:fp * S + (Nb + 1) * 512]),
                            [E(ke)], sdeps=[at_state["sq_guard"]])
                        prev_rd = None
                        for hh in range(2):
                            qm = tr.add("tensor", lambda e, hh=hh: e.matmul(
                                PD[0:1, 0:512], ONEBC[hh * 64:hh * 64 + 64, 0:1],
                                QSQ[hh * 64:hh * 64 + 64, :], start=True, stop=True),
                                [E(sq), E(at_state["pe_prev"]), E(prev_rd)])
                            rd = tr.add("vector", lambda e: e.tensor_reduce(
                                ROWS[0:1, 32:33], PD[0:1, 0:512], AX.X, ALU.max),
                                [E(qm)], sdeps=[sq, ksq_prev])
                            prev_rd = rd
                            mx = tr.add("vector", lambda e, fp=fp, hh=hh:
                                        e.tensor_tensor(
                                ROWS[0:1, 2 * fp + hh:2 * fp + hh + 1],
                                ROWS[0:1, 2 * fp + hh:2 * fp + hh + 1],
                                ROWS[0:1, 32:33], ALU.max), [E(k2m_init)],
                                sdeps=[rd])
                            ksq_prev = mx
                            at_state["sq_guard"] = mx
                    kw_consumed = ke
                at_state["mx_prev"] = ksq_prev
                mk1 = tr.add("scalar", lambda e: e.activation(
                    ROWS[0:1, 16:24], ROWS[0:1, 0:8], AF.Ln), [E(ksq_prev)])
                mk2 = tr.add("scalar", lambda e: e.activation(
                    ROWS[0:1, 16:24], ROWS[0:1, 16:24], AF.Exp, scale=0.5), [],
                    sdeps=[mk1])

                vw = tr.dma("gpsimd", lambda e, hg=hg: e.dma_start(
                    WQ2B[:, 0:4 * D].rearrange("p (dt c) -> p dt c", c=512),
                    w_qkv.rearrange("(dt p) c -> p dt c", p=128)
                    [:, :, 2 * D + hg * 512:2 * D + (hg + 1) * 512]), sem_qk[1],
                    [E(kt_last_evac)] + guard)
                vones = tr.add("vector", lambda e: e.memset(
                    ARBB[:, 0:NKT * VBLK].rearrange("p (t c) -> p t c", c=VBLK)
                    [:, :, 64:VBLK:65], 1.0), guard, sdeps=[at_state["mx_prev"]])
                v_evacs = []
                for tt in range(NKT):
                    vm = None
                    for dt in range(NDT):
                        waits = []
                        if dt == 0:
                            waits = [("d", vw.dma_sem, vw.post, vw), E(vones),
                                     E(ksq_prev)]
                            if tt >= 2:
                                waits.append(E(v_evacs[tt - 2]))
                        vm = tr.add("tensor", lambda e, tt=tt, dt=dt: e.matmul(
                            PC[:, 0:512] if tt % 2 == 0 else PD[:, 0:512],
                            L2T[:, dt * S + tt * 128:dt * S + (tt + 1) * 128],
                            WQ2B[:, dt * 512:(dt + 1) * 512],
                            start=(dt == 0), stop=(dt == NDT - 1)), waits)
                    ve = tr.add("vector", lambda e, tt=tt, hg=hg: e.tensor_add(
                        ARBB[:, tt * VBLK:(tt + 1) * VBLK]
                        .rearrange("p (h c) -> p h c", c=65)[:, :, 0:64],
                        (PC[:, 0:512] if tt % 2 == 0 else PD[:, 0:512])
                        .rearrange("p (h c) -> p h c", c=64),
                        BQVB[:, hg * 512:(hg + 1) * 512]
                        .rearrange("p (h c) -> p h c", c=64)), [E(vm)],
                        sdeps=[vones] if tt == 0 else [])
                    v_evacs.append(ve)

                qw_consumed = None
                wo_dma = None
                for qb in range(NQB):
                    qt_last = None
                    negb_last = None
                    for fp in range(FPG):
                        pair = hg * FPG + fp
                        qw = tr.dma("gpsimd", lambda e, pair=pair: e.dma_start(
                            WQ1B[:, D:2 * D].rearrange("p (dt c) -> p dt c", c=128),
                            w_qkv.rearrange("(dt p) c -> p dt c", p=128)
                            [:, :, pair * 128:(pair + 1) * 128]), sem_qk[0],
                            [E(qw_consumed), E(v_evacs[-1])])
                        qm = None
                        for dt in range(NDT):
                            waits = []
                            if dt == 0:
                                waits = [("d", qw.dma_sem, qw.post, qw), E(qt_last),
                                         E(v_evacs[-1]), E(at_state["pe_prev"])]
                            qm = tr.add("tensor", lambda e, fp=fp, dt=dt, qb=qb:
                                        e.matmul(
                                PC[:, 0:512], WQ1B[:, D + dt * 128:D + (dt + 1) * 128],
                                L2T[:, dt * S + qb * 512:dt * S + (qb + 1) * 512],
                                start=(dt == 0), stop=(dt == NDT - 1)), waits)
                        qw_consumed = qm
                        qe = tr.add("scalar", lambda e, fp=fp, pair=pair: e.activation(
                            ARBB[:, QT_OFF + fp * 512:QT_OFF + (fp + 1) * 512],
                            PC[:, 0:512], AF.Identity,
                            bias=BQCOL[:, pair:pair + 1]), [E(qm)])
                        qt_last = qe
                        sq = tr.add("vector", lambda e, fp=fp: e.tensor_mul(
                            QSQ[:], ARBB[:, QT_OFF + fp * 512:QT_OFF + (fp + 1) * 512],
                            ARBB[:, QT_OFF + fp * 512:QT_OFF + (fp + 1) * 512]),
                            [E(qe)], sdeps=[at_state["sq_guard"]])
                        prev_rd = None
                        for hh in range(2):
                            idx = 2 * fp + hh
                            qm2 = tr.add("tensor", lambda e, hh=hh: e.matmul(
                                PD[0:1, 0:512], ONEBC[hh * 64:hh * 64 + 64, 0:1],
                                QSQ[hh * 64:hh * 64 + 64, :], start=True, stop=True),
                                [E(sq), E(at_state["pe_prev"]), E(prev_rd),
                                 E(at_state["s2_guard"])])
                            rd = tr.add("vector", lambda e, idx=idx: e.tensor_reduce(
                                ROWS[0:1, 40 + idx:41 + idx], PD[0:1, 0:512], AX.X,
                                ALU.max), [E(qm2)], sdeps=[sq])
                            prev_rd = rd
                            at_state["sq_guard"] = rd
                            negb_last = rd
                    bl1 = tr.add("scalar", lambda e: e.activation(
                        ROWS[0:1, 48:56], ROWS[0:1, 40:48], AF.Ln), [E(negb_last)],
                        sdeps=[at_state["s2_guard"]])
                    bl2 = tr.add("scalar", lambda e: e.activation(
                        ROWS[0:1, 48:56], ROWS[0:1, 48:56], AF.Exp, scale=0.5), [],
                        sdeps=[bl1])
                    at_state["s2_guard"] = bl2
                    bl3 = tr.add("vector", lambda e: e.tensor_mul(
                        ROWS[0:1, 40:48], ROWS[0:1, 48:56], ROWS[0:1, 16:24]),
                        [E(bl2), E(mk2)], sdeps=[negb_last])
                    bl4 = tr.add("vector", lambda e: e.tensor_scalar(
                        ROWS[0:1, 40:48], ROWS[0:1, 40:48], -0.125, None, ALU.mult),
                        [], sdeps=[bl3])
                    nbm = tr.add("tensor", lambda e: e.matmul(
                        PC[:, 0:8], ONEFR[0:1, :], ROWS[0:1, 40:48], start=True,
                        stop=True), [E(bl4), E(qt_last)])
                    nbc = tr.add("scalar", lambda e: e.copy(NEGB[:], PC[:, 0:8]),
                                 [E(nbm)], sdeps=[qt_last])

                    for fp in range(FPG):
                        av_last = {0: None, 1: None}
                        exp_ops = {}
                        av = None
                        for g in range(8):
                            eset = g % 2
                            for hh in range(2):
                                sc_last = None
                                for kk in range(2):
                                    kt = 2 * g + kk
                                    waits = []
                                    if kk == 0:
                                        if g >= 1:
                                            waits = [E(exp_ops[(g - 1, hh)])]
                                        else:
                                            waits = [E(nbc), E(at_state["exp_prev"])]
                                    sc_last = tr.add("tensor", lambda e, fp=fp,
                                                     kt=kt, hh=hh, kk=kk: e.matmul(
                                        (PS1 if hh == 0 else PS2)
                                        [:, kk * 512:(kk + 1) * 512],
                                        KT[hh * 64:hh * 64 + 64,
                                           fp * S + kt * 128:fp * S + (kt + 1) * 128],
                                        ARBB[hh * 64:hh * 64 + 64,
                                             QT_OFF + fp * 512:QT_OFF + (fp + 1) * 512],
                                        start=True, stop=True,
                                        tile_position=(hh * 64, 0)), waits)
                                ew = [E(sc_last)]
                                if g >= 2:
                                    ew.append(E(av_last[eset]))
                                elif at_state["av_prev"] is not None:
                                    ew.append(at_state["av_prev"])
                                exp_ops[(g, hh)] = tr.add(
                                    "scalar", lambda e, fp=fp, hh=hh, eset=eset:
                                    e.activation(
                                        ARBB[:, E_OFF + eset * 2048 + hh * 1024:
                                             E_OFF + eset * 2048 + (hh + 1) * 1024],
                                        (PS1 if hh == 0 else PS2)[:],
                                        AF.Exp, scale=0.125,
                                        bias=NEGB[:, 2 * fp + hh:2 * fp + hh + 1]),
                                    ew, sdeps=[nbc] if (g == 0 and hh == 0) else [])
                            for kk in range(2):
                                kt = 2 * g + kk
                                for hh in range(2):
                                    waits = []
                                    if kk == 0:
                                        waits = [E(exp_ops[(g, 0)]),
                                                 E(exp_ops[(g, 1)])]
                                        if g == 0 and hh == 0 and at_state["pair_evacs"]:
                                            waits += at_state["pair_evacs"]
                                    av = tr.add("tensor", lambda e, fp=fp, kt=kt,
                                                hh=hh, eset=eset, kk=kk: e.matmul(
                                        (PA if hh == 0 else PB)[0:65, :],
                                        ARBB[:, kt * VBLK + (2 * fp + hh) * 65:
                                             kt * VBLK + (2 * fp + hh) * 65 + 65],
                                        ARBB[:, E_OFF + eset * 2048 + hh * 1024 + kk * 512:
                                             E_OFF + eset * 2048 + hh * 1024 + (kk + 1) * 512],
                                        start=(kt == 0), stop=(kt == NKT - 1)), waits)
                            av_last[eset] = av
                        at_state["exp_prev"] = exp_ops[(7, 1)]
                        at_state["av_prev"] = E(av)

                        pair_evacs = []
                        shift_ev = None
                        for hh in range(2):
                            ps = PA if hh == 0 else PB
                            rc_ = tr.add("vector", lambda e, ps=ps: e.reciprocal(
                                ROWR[0:1, :], ps[64:65, :]), [E(av)],
                                sdeps=[at_state["rowb_guard"]])
                            rb_ = tr.add("scalar", lambda e: e.copy(
                                ROWB[0:1, :], ROWR[0:1, :]), [E(rc_)],
                                sdeps=[at_state["bc_prev"]])
                            bm = tr.add("tensor", lambda e: e.matmul(
                                PC[0:64, :], ONEBR[0:1, 0:64], ROWB[0:1, :],
                                start=True, stop=True),
                                [E(rb_), E(at_state["bc_prev"])])
                            bc_ = tr.add("scalar", lambda e: e.copy(
                                BCS[:], PC[0:64, :]), [E(bm)], sdeps=[rb_])
                            at_state["bc_prev"] = bc_
                            if hh == 0:
                                mu = tr.add("vector", lambda e, fp=fp, ps=ps:
                                            e.tensor_mul(
                                    AVT[0:64, fp * QB:(fp + 1) * QB], ps[0:64, :],
                                    BCS[:]), [E(bc_)], sdeps=[rc_])
                            else:
                                mu = tr.add("vector", lambda e, ps=ps: e.tensor_mul(
                                    AVST[:], ps[0:64, :], BCS[:]),
                                    [E(bc_), at_state["shift_prev"]],
                                    sdeps=[rc_])
                                shift_ev = tr.dma("gpsimd", lambda e, fp=fp:
                                                  e.dma_start(
                                    AVT[64:128, fp * QB:(fp + 1) * QB], AVST[:]),
                                    sem_sh, [E(mu)])
                                at_state["shift_prev"] = (
                                    "d", shift_ev.dma_sem, shift_ev.post, shift_ev)
                            at_state["rowb_guard"] = bc_
                            pair_evacs.append(E(mu))
                        if shift_ev is not None:
                            pair_evacs.append(("d", shift_ev.dma_sem, shift_ev.post,
                                               shift_ev))
                        at_state["pair_evacs"] = pair_evacs

                    if qb == 0:
                        wo_dma = tr.dma("gpsimd", lambda e, hg=hg: e.dma_start(
                            WQ2B[:, 4 * D:8 * D].rearrange("p (dt c) -> p dt c", c=D),
                            w_out.rearrange("(dt p) c -> p dt c", p=128)
                            [:, hg * 4:hg * 4 + 4, :]), sem_qk[1],
                            [E(v_evacs[-1])])
                    for tt in range(4):
                        for fb in range(2):
                            pm = None
                            for fp in range(FPG):
                                waits = []
                                if fp == 0:
                                    waits = [("d", wo_dma.dma_sem, wo_dma.post, wo_dma),
                                             E(at_state["pe_prev"])]
                                    if tt == 0 and fb == 0:
                                        waits += at_state["pair_evacs"]
                                pm = tr.add("tensor", lambda e, tt=tt, fb=fb, fp=fp:
                                            e.matmul(
                                    PD[:, 0:512],
                                    AVT[:, fp * QB + tt * 128:fp * QB + (tt + 1) * 128],
                                    WQ2B[:, 4 * D + fp * D + fb * 512:
                                         4 * D + fp * D + (fb + 1) * 512],
                                    start=(fp == 0), stop=(fp == FPG - 1)), waits)
                            gtt = qb * 4 + tt
                            pe = tr.add("vector", lambda e, gtt=gtt, fb=fb:
                                        e.scalar_tensor_tensor(
                                X[:, gtt * D + fb * 512:gtt * D + (fb + 1) * 512],
                                PD[:, 0:512], ONEF[:, 0:1],
                                X[:, gtt * D + fb * 512:gtt * D + (fb + 1) * 512],
                                ALU.mult, ALU.add),
                                [E(pm)], sdeps=[preadd[gtt]])
                            at_state["pe_prev"] = pe
                            last_proj.append(pe)
            at_state["xread_guard"] = last_proj[-1]
            return last_proj

        xread_sdep = None
        for b in range(nb):
            xw = [] if b == 0 else [("d", sem_y, sem_y.count * 16, None)]
            for tt in range(NT):
                tr.dma("sync", lambda e, b=b, tt=tt: e.dma_start(
                    X[:, tt * D:(tt + 1) * D],
                    x_in.rearrange("(n p) c -> n p c", p=128)[b * NT + tt]),
                    sem_x, xw if tt == 0 else [])
            x_ready = [("d", sem_x, sem_x.count * 16, None)]
            arena_free = list(prep_done) if b == 0 else []
            if "f1" in phases:
                evs = ffn(x_ready, arena_free, xread_sdep)
                x_ready = [E(o) for o in evs[-8:]]
                xread_sdep = evs[-1]
                arena_free = []
            if "at" in phases:
                evs = attn(x_ready, arena_free, xread_sdep)
                x_ready = [E(o) for o in evs[-8:]]
                xread_sdep = evs[-1]
                arena_free = []
            if "f2" in phases:
                evs = ffn(x_ready, arena_free, xread_sdep)
                x_ready = [E(o) for o in evs[-8:]]
                xread_sdep = evs[-1]
            for tt in range(NT):
                tr.dma("sync", lambda e, b=b, tt=tt: e.dma_start(
                    y_out.rearrange("(n p) c -> n p c", p=128)[b * NT + tt],
                    X[:, tt * D:(tt + 1) * D]), sem_y, x_ready if tt == 0 else [])

        tr.add("sync", lambda e: e.nop(), [("d", sem_y, sem_y.count * 16, None)])
        tr.emit(nc, ctx)
    nc._dbg_ndrains = tr.n_drains
    return nc


_cache = {}


def _get_program(phases=("f1", "at", "f2")):
    key = tuple(phases)
    if key not in _cache:
        _cache[key] = build_program(phases)
    return _cache[key]


def _host_layouts(L0_attn_w_qkv, L0_attn_b_qkv, L0_attn_w_out, L0_attn_b_out,
                  L0_ffn_w1, L0_ffn_conv_w, L0_ffn_conv_b, L0_ffn_w2):
    wq = np.ascontiguousarray(np.asarray(L0_attn_w_qkv, np.float32))
    bq = np.asarray(L0_attn_b_qkv, np.float32).reshape(3 * D)
    bqcol = np.ascontiguousarray(bq[:3072].reshape(24, 128).T)
    bqv_row = np.ascontiguousarray(bq[2 * D:3 * D].reshape(1, D))
    bout_row = np.ascontiguousarray(np.asarray(L0_attn_b_out, np.float32).reshape(1, D))
    wo = np.ascontiguousarray(np.asarray(L0_attn_w_out, np.float32))
    w1 = np.ascontiguousarray(np.asarray(L0_ffn_w1, np.float32))
    cw = np.asarray(L0_ffn_conv_w, np.float32).reshape(HID, 3)
    cw_pt = np.ascontiguousarray(cw.reshape(NCT, 128, 3).transpose(1, 0, 2)
                                 .reshape(128, NCT * 3))
    cb = np.asarray(L0_ffn_conv_b, np.float32).reshape(HID)
    cb_pt = np.ascontiguousarray(cb.reshape(NCT, 128).T)
    w2 = np.ascontiguousarray(np.asarray(L0_ffn_w2, np.float32))
    return dict(w_qkv=wq, bqcol=bqcol, bqv_row=bqv_row, bout_row=bout_row,
                w_out=wo, w1=w1, cw_pt=cw_pt, cb_pt=cb_pt, w2=w2)


def kernel(L0_attn_x, L0_attn_w_qkv, L0_attn_b_qkv, L0_attn_w_out, L0_attn_b_out,
           L0_ffn_w1, L0_ffn_conv_w, L0_ffn_conv_b, L0_ffn_w2):
    x = np.ascontiguousarray(np.asarray(L0_attn_x, dtype=np.float32)).reshape(16 * S, D)
    shards = np.split(x, NCORES, axis=0)
    wmap = _host_layouts(L0_attn_w_qkv, L0_attn_b_qkv, L0_attn_w_out, L0_attn_b_out,
                         L0_ffn_w1, L0_ffn_conv_w, L0_ffn_conv_b, L0_ffn_w2)
    nc = _get_program()
    in_maps = [dict(x=s, **wmap) for s in shards]
    res = run_bass_kernel_spmd(nc, in_maps, core_ids=list(range(NCORES)))
    y = np.concatenate([r["y"] for r in res.results], axis=0)
    return y.reshape(16, S, D)


# version 04 (final)
# speedup vs baseline: 1.2206x; 1.1573x over previous
import numpy as np
import ml_dtypes
import concourse.bass as bass
import concourse.mybir as mybir
from concourse.bass_utils import run_bass_kernel_spmd
from contextlib import ExitStack

F32 = mybir.dt.float32
BF16 = mybir.dt.bfloat16
AF = mybir.ActivationFunctionType
ALU = mybir.AluOpType
AX = mybir.AxisListType

RC = 1.5 * 2 ** 23
EPS = 1e-5
NCORES = 8
NB = 2
S = 2048
D = 1024
HID = 4096
NT = S // 128
NDT = D // 128
NCT = HID // 128
WIN = 256
NWIN = S // WIN
QB = 512
NQB = S // QB
NKT = S // 128
NHG = 2
FPG = 4


class _Op:
    __slots__ = ("eng", "fn", "waits", "flag", "dma_sem", "post", "val",
                 "pos", "know", "pre_drain")

    def __init__(self, eng, fn, waits):
        self.eng = eng
        self.fn = fn
        self.waits = [w for w in waits if w is not None]
        self.flag = False
        self.dma_sem = None
        self.post = 0
        self.val = 0
        self.pos = 0
        self.know = {}
        self.pre_drain = False


class _Sem:
    __slots__ = ("name", "handle", "count")

    def __init__(self, name):
        self.name = name
        self.handle = None
        self.count = 0


class Trace:
    ENGS = ("sync", "gpsimd", "tensor", "vector", "scalar")

    def __init__(self):
        self.ops = []
        self.esem = {e: _Sem("es_" + e) for e in self.ENGS}
        self.dsems = {}
        self.eng_pos = {e: 0 for e in self.ENGS}
        self.eng_know = {e: {} for e in self.ENGS}
        self.n_drains = 0

    def dma_sem(self, name):
        if name not in self.dsems:
            self.dsems[name] = _Sem("ds_" + name)
        return self.dsems[name]

    def add(self, eng, fn, waits=(), sdeps=()):
        op = _Op(eng, fn, waits)
        self.eng_pos[eng] += 1
        op.pos = self.eng_pos[eng]
        know = dict(self.eng_know[eng])
        sdeps = list(sdeps)

        def join(prod):
            for k, v in prod.know.items():
                if know.get(k, -1) < v:
                    know[k] = v
            if know.get(prod.eng, -1) < prod.pos:
                know[prod.eng] = prod.pos

        for w in op.waits:
            if w[0] == "e":
                if w[1].eng == eng:
                    sdeps.append(w[1])
                else:
                    join(w[1])
            elif w[0] == "d" and len(w) > 3 and w[3] is not None:
                join(w[3])
        need = any(d is not None and d.pos > know.get(eng, 0) for d in sdeps)
        if need:
            op.pre_drain = True
            self.n_drains += 1
            know[eng] = op.pos - 1
        op.know = know
        self.eng_know[eng] = know
        self.ops.append(op)
        return op

    def dma(self, eng, fn, sem, waits=(), sdeps=()):
        w = [x for x in waits if x is not None]
        if sem.count > 0:
            w.append(("d", sem, sem.count * 16, None))
        op = self.add(eng, fn, w, sdeps)
        sem.count += 1
        op.dma_sem = sem
        op.post = sem.count * 16
        return op

    @staticmethod
    def ev(op):
        if op is None:
            return None
        if op.dma_sem is not None:
            return ("d", op.dma_sem, op.post, op)
        op.flag = True
        return ("e", op)

    def emit(self, nc, ctx):
        cnt = {e: 0 for e in self.esem}
        for op in self.ops:
            if op.dma_sem is None and op.flag:
                cnt[op.eng] += 1
                op.val = cnt[op.eng]
        for s in list(self.esem.values()) + list(self.dsems.values()):
            s.handle = ctx.enter_context(nc.semaphore(s.name))
        block = ctx.enter_context(nc.Block())

        def make_body(ename):
            my = [op for op in self.ops if op.eng == ename]

            def body(e):
                waited = {}
                for op in my:
                    if op.pre_drain:
                        e.drain()
                    for w in op.waits:
                        if w[0] == "d":
                            sem, val = w[1], w[2]
                        else:
                            if w[1].eng == ename:
                                continue
                            sem, val = self.esem[w[1].eng], w[1].val
                        if waited.get(id(sem), -1) < val:
                            waited[id(sem)] = val
                            e.wait_ge(sem.handle, val)
                    ins = op.fn(e)
                    if op.dma_sem is not None:
                        ins.then_inc(op.dma_sem.handle, 16)
                    elif op.flag:
                        ins.then_inc(self.esem[op.eng].handle, 1)
            return body

        for ename in self.ENGS:
            if any(op.eng == ename for op in self.ops):
                getattr(block, ename)(make_body(ename))


E = Trace.ev


def build_program(phases=("f1", "at", "f2"), nb=NB):
    nc = bass.Bass("TRN2", target_bir_lowering=False, debug=False)

    x_in = nc.dram_tensor("x", [nb * S, D], F32, kind="ExternalInput").ap()
    w_qkv = nc.dram_tensor("w_qkv", [D, 3 * D], F32, kind="ExternalInput").ap()
    bqcol = nc.dram_tensor("bqcol", [128, 24], F32, kind="ExternalInput").ap()
    bqv_row = nc.dram_tensor("bqv_row", [1, D], F32, kind="ExternalInput").ap()
    bout_row = nc.dram_tensor("bout_row", [1, D], F32, kind="ExternalInput").ap()
    w_out = nc.dram_tensor("w_out", [D, D], F32, kind="ExternalInput").ap()
    w1 = nc.dram_tensor("w1", [HID, D], F32, kind="ExternalInput").ap()
    cw_in = nc.dram_tensor("cw_pt", [128, 3 * NCT], F32, kind="ExternalInput").ap()
    cb_in = nc.dram_tensor("cb_pt", [128, NCT], F32, kind="ExternalInput").ap()
    w2 = nc.dram_tensor("w2", [D, HID], F32, kind="ExternalInput").ap()
    y_out = nc.dram_tensor("y", [nb * S, D], F32, kind="ExternalOutput").ap()

    wq1t = nc.dram_tensor("wq1t_scr", [NCT, 128, D], BF16, kind="Internal").ap()
    wq2t = nc.dram_tensor("wq2t_scr", [NCT, 128, D], BF16, kind="Internal").ap()
    ident = nc.inline_tensor(np.eye(128, dtype=np.float32), name="identf32").ap()

    tr = Trace()

    with ExitStack() as ctx:
        def sb(name, shape, dtype):
            return ctx.enter_context(nc.sbuf_tensor(name, shape, dtype))

        X = sb("X", [128, NT * D], F32)
        CW = sb("CW", [128, NCT * 3], F32)
        CB = sb("CB", [128, NCT], F32)
        BQCOL = sb("BQCOL", [128, 24], F32)
        IDF = sb("IDF", [128, 128], F32)
        ONEF = sb("ONEF", [128, 1], F32)
        NEGONE = sb("NEGONE", [128, 1], F32)
        EPSC = sb("EPSC", [128, 1], F32)
        ONEFR = sb("ONEFR", [1, 128], F32)
        ONEBR = sb("ONEBR", [1, 128], BF16)
        ONEBC = sb("ONEBC", [128, 1], BF16)
        RCCOL = sb("RCCOL", [128, 1], F32)
        NRCCOL = sb("NRCCOL", [128, 1], F32)
        NRCB = sb("NRCB", [128, WIN], F32)
        WSC = sb("WSC", [128, 4], F32)
        BQVB = sb("BQVB", [128, D], F32)
        BOUTB = sb("BOUTB", [128, D], F32)
        STATS = sb("STATS", [128, 64], F32)
        AMAX1 = sb("AMAX1", [128, NT], F32)
        S1C = sb("S1C", [128, NT], F32)
        F1C = sb("F1C", [128, NT], F32)
        AMAX2 = sb("AMAX2", [128, NT], F32)
        S2CC = sb("S2CC", [128, NT], F32)
        FAC2 = sb("FAC2", [128, NT], F32)
        SCR16 = sb("SCR16", [128, NT], F32)
        F1ROW = sb("F1ROW", [1, S], F32)
        S2ROW = sb("S2ROW", [1, WIN], F32)
        ROWS = sb("ROWS", [1, 64], F32)

        AR_A = sb("AR_A", [128, NDT * S], BF16)
        AR_B = sb("AR_B", [128, NCT * WIN], F32)
        AR_C = sb("AR_C", [128, NCT * WIN], BF16)
        WQ1B = sb("WQ1B", [128, 4 * D], BF16)
        WQ2B = sb("WQ2B", [128, 8 * D], BF16)
        HS = sb("HS", [128, 2 * (WIN + 2)], F32)
        TA = sb("TA", [128, 2 * WIN], F32)
        TB = sb("TB", [128, 2 * WIN], F32)
        F1B = sb("F1B", [128, WIN + 2], F32)
        S2B = sb("S2B", [128, WIN], F32)
        ACC = sb("ACC", [128, WIN], F32)
        QR = sb("QR", [128, 2 * WIN], F32)
        BST = sb("BST", [128, 12], F32)
        AVT = sb("AVT", [128, FPG * QB], BF16)
        NEGB = sb("NEGB", [128, 8], F32)
        BCS = TA.bitcast(F32)[0:64, 0:QB]
        AVST = HS.bitcast(BF16)[0:64, 0:QB]
        QSQ = F1B.bitcast(BF16)[:, 0:QB]
        ROWR = TB[0:1, 0:QB]
        ROWB = ACC.bitcast(BF16)[0:1, 0:QB]

        LN1 = AR_C.bitcast(F32)
        QRL = AR_C.bitcast(F32)
        XQ = AR_B.bitcast(BF16)
        H2 = AR_B
        XQT = AR_A
        XQ2 = AR_C
        L2T = AR_A
        L2TOK = AR_B.bitcast(BF16)
        ARBB = AR_B.bitcast(BF16)
        KT = AR_C
        VBLK = 8 * 65
        E_OFF = NKT * VBLK
        QT_OFF = E_OFF + 4096

        PS1 = ctx.enter_context(nc.psum_tensor("PS1", [128, 1024], F32))
        PS2 = ctx.enter_context(nc.psum_tensor("PS2", [128, 1024], F32))
        PA = ctx.enter_context(nc.psum_tensor("PA", [128, 512], F32))
        PB = ctx.enter_context(nc.psum_tensor("PB", [128, 512], F32))
        PC = ctx.enter_context(nc.psum_tensor("PC", [128, 512], F32))
        PD = ctx.enter_context(nc.psum_tensor("PD", [128, 512], F32))

        sem_x = tr.dma_sem("xld")
        sem_y = tr.dma_sem("yst")
        sem_m1 = [tr.dma_sem("wq1_%d" % i) for i in range(3)]
        sem_m2 = [tr.dma_sem("wq2_%d" % i) for i in range(3)]
        sem_tp = [tr.dma_sem("tp_%d" % i) for i in range(2)]
        sem_wl = [tr.dma_sem("wl_%d" % i) for i in range(4)]
        sem_sw = [tr.dma_sem("sw_%d" % i) for i in range(2)]
        sem_qk = [tr.dma_sem("qk_%d" % i) for i in range(2)]
        sem_sh = tr.dma_sem("shift")

        tr.add("vector", lambda e: e.memset(ONEF[:], 1.0))
        tr.add("vector", lambda e: e.memset(NEGONE[:], -1.0))
        tr.add("vector", lambda e: e.memset(EPSC[:], EPS))
        tr.add("vector", lambda e: e.memset(ONEFR[:], 1.0))
        tr.add("vector", lambda e: e.memset(ONEBR[:], 1.0))
        tr.add("vector", lambda e: e.memset(ONEBC[:], 1.0))
        tr.add("vector", lambda e: e.memset(RCCOL[:], RC))
        op_consts = tr.add("vector", lambda e: e.memset(NRCCOL[:], -RC))
        op_nrcb = tr.add("vector", lambda e: e.memset(NRCB[:], -RC))

        ld_id = tr.dma("sync", lambda e: e.dma_start(IDF[:], ident), sem_wl[0])
        ld_cw = tr.dma("sync", lambda e: e.dma_start(CW[:], cw_in), sem_wl[0])
        ld_cb = tr.dma("sync", lambda e: e.dma_start(CB[:], cb_in), sem_wl[0])
        ld_bq = tr.dma("sync", lambda e: e.dma_start(BQCOL[:], bqcol), sem_wl[0])
        ld_bv = tr.dma("sync", lambda e: e.dma_start(F1ROW[0:1, 0:D], bqv_row), sem_wl[0])
        ld_bo = tr.dma("sync", lambda e: e.dma_start(F1ROW[0:1, D:2 * D], bout_row), sem_wl[0])

        bc_prev = None
        for (dst, base) in ((BQVB, 0), (BOUTB, D)):
            for c0 in (0, 512):
                m = tr.add("tensor", lambda e, base=base, c0=c0: e.matmul(
                    PC[:, 0:512], ONEFR[0:1, :], F1ROW[0:1, base + c0:base + c0 + 512],
                    start=True, stop=True),
                    [E(ld_bv), E(ld_bo), E(op_consts), E(bc_prev)])
                bc_prev = tr.add("scalar", lambda e, dst=dst, c0=c0: e.copy(
                    dst[:, c0:c0 + 512], PC[:, 0:512]), [E(m)],
                    sdeps=[bc_prev])
        op_bias_b = bc_prev

        W1LD = AR_B
        WQBF = AR_C
        STG = AR_A
        PSUMW = STATS

        def prep_matrix(src_ap, nblk, blk_cols, nslot, ws_col, inv_col, inv_scale,
                        scratch, is_w1, rows_c0, prev_events):
            reds = []
            for i in range(nblk):
                w = list(prev_events) if i < nslot else [E(reds[i - nslot])]
                q = "sync" if i % 2 == 0 else "gpsimd"
                ld = tr.dma(q, lambda e, i=i, nslot=nslot: e.dma_start(
                    W1LD[:, (i % nslot) * blk_cols:(i % nslot + 1) * blk_cols],
                    src_ap.rearrange("(n p) c -> n p c", p=128)[i]),
                    sem_wl[i % len(sem_wl)], w)
                red = tr.add("vector", lambda e, i=i, nslot=nslot: e.tensor_reduce(
                    PSUMW[:, i:i + 1],
                    W1LD[:, (i % nslot) * blk_cols:(i % nslot + 1) * blk_cols],
                    AX.X, ALU.add, apply_absolute_value=True), [E(ld)])
                reds.append(red)
            redall = tr.add("vector", lambda e: e.tensor_reduce(
                PSUMW[:, 40:41], PSUMW[:, 0:nblk], AX.X, ALU.add), [],
                sdeps=[reds[-1]])
            msum = tr.add("tensor", lambda e: e.matmul(
                PC[0:1, 0:1], PSUMW[:, 40:41], ONEF[:, 0:1], start=True, stop=True),
                [E(redall)] + list(prev_events))
            c0 = rows_c0
            t1 = tr.add("scalar", lambda e: e.copy(ROWS[0:1, c0:c0 + 1], PC[0:1, 0:1]),
                        [E(msum)])
            t2 = tr.add("vector", lambda e: e.tensor_scalar(
                ROWS[0:1, c0 + 1:c0 + 2], ROWS[0:1, c0:c0 + 1], 1.0 / float(HID * D),
                1e-5, ALU.mult, ALU.max), [E(t1)])
            t3 = tr.add("vector", lambda e: e.reciprocal(
                ROWS[0:1, c0 + 2:c0 + 3], ROWS[0:1, c0 + 1:c0 + 2]), [], sdeps=[t2])
            t4 = tr.add("vector", lambda e: e.reciprocal(
                ROWS[0:1, c0 + 3:c0 + 4], ROWS[0:1, c0 + 2:c0 + 3]), [], sdeps=[t3])
            t4b = tr.add("vector", lambda e: e.tensor_scalar(
                ROWS[0:1, c0 + 4:c0 + 5], ROWS[0:1, c0 + 3:c0 + 4], inv_scale, None,
                ALU.mult), [], sdeps=[t4])
            mws = tr.add("tensor", lambda e: e.matmul(
                PC[:, 0:1], ONEFR[0:1, :], ROWS[0:1, c0 + 2:c0 + 3], start=True,
                stop=True), [E(t4b), E(t1)])
            cws = tr.add("scalar", lambda e: e.copy(
                WSC[:, ws_col:ws_col + 1], PC[:, 0:1]), [E(mws)], sdeps=[t1])
            minv = tr.add("tensor", lambda e: e.matmul(
                PC[:, 1:2], ONEFR[0:1, :], ROWS[0:1, c0 + 4:c0 + 5], start=True,
                stop=True), [E(cws)])
            cinv = tr.add("scalar", lambda e: e.copy(
                WSC[:, inv_col:inv_col + 1], PC[:, 1:2]), [E(minv)], sdeps=[cws])
            prev_q = {}
            stores = []
            tps = []
            for i in range(nblk):
                wslot = (i % nslot) * blk_cols
                q = "sync" if i % 2 == 0 else "gpsimd"
                ld = tr.dma(q, lambda e, wslot=wslot, i=i: e.dma_start(
                    W1LD[:, wslot:wslot + blk_cols],
                    src_ap.rearrange("(n p) c -> n p c", p=128)[i]),
                    sem_wl[i % len(sem_wl)],
                    [E(prev_q.get(i % nslot, redall))])
                q1 = tr.add("vector", lambda e, wslot=wslot: e.tensor_scalar(
                    W1LD[:, wslot:wslot + blk_cols], W1LD[:, wslot:wslot + blk_cols],
                    WSC[:, ws_col:ws_col + 1], RC, ALU.mult, ALU.add),
                    [E(ld), E(cinv)] + (list(prev_events) if i == 0 else []))
                q2 = tr.add("scalar", lambda e, wslot=wslot: e.activation(
                    W1LD[:, wslot:wslot + blk_cols], W1LD[:, wslot:wslot + blk_cols],
                    AF.Identity, bias=NRCCOL[:, 0:1]), [E(q1)])
                q3 = tr.add("vector", lambda e, wslot=wslot: e.tensor_scalar(
                    WQBF[:, wslot:wslot + blk_cols], W1LD[:, wslot:wslot + blk_cols],
                    -1.0, 1.0, ALU.max, ALU.min),
                    [E(q2)] + (list(prev_events) if i == 0 else [])
                    + ([E(tps[i - nslot])] if i >= nslot else []))
                prev_q[i % nslot] = q3
                ntile = blk_cols // 128
                tp = tr.dma("scalar", lambda e, wslot=wslot, ntile=ntile:
                            e.dma_start_transpose(
                    STG[:, 0:blk_cols].rearrange("p (n f) -> p n f", n=ntile),
                    WQBF[:, wslot:wslot + blk_cols].rearrange("p (n f) -> p n f",
                                                              n=ntile)),
                    sem_tp[i % 2],
                    [E(q3)] + ([E(stores[-1])] if stores else list(prev_events)))
                tps.append(tp)
                if is_w1:
                    stx = tr.dma("sync", lambda e, i=i: e.dma_start(
                        scratch[i], STG[:, 0:blk_cols]), sem_sw[0], [E(tp)])
                else:
                    stx = tr.dma("sync", lambda e, i=i: e.dma_start(
                        scratch[:, :, i * 128:(i + 1) * 128]
                        .rearrange("ct p fm -> p ct fm"),
                        STG[:, 0:blk_cols].rearrange("p (ct fm) -> p ct fm", fm=128)),
                        sem_sw[1], [E(tp)])
                stores.append(stx)
            return stores, q3, cinv

        w1_stores, w1_lastq, w1_cinv = prep_matrix(
            w1, NCT, D, 4, 2, 0, 1.0, wq1t, True, 8, [E(ld_id), E(op_bias_b)])
        w2_stores, w2_lastq, w2_cinv = prep_matrix(
            w2, NDT, HID, 2, 3, 1, 0.5, wq2t, False, 16,
            [E(w1_lastq), E(w1_stores[-1])])
        w1_done = ("d", tr.dsems["sw_0"], tr.dsems["sw_0"].count * 16, w1_stores[-1])
        w2_done = ("d", tr.dsems["sw_1"], tr.dsems["sw_1"].count * 16, w2_stores[-1])
        prep_done = [w1_done, w2_done, E(w2_lastq)]

        def layernorm_tile(tt, out_ap, waits, sdeps=()):
            s1 = tr.add("vector", lambda e, tt=tt: e.bn_stats(
                BST[:, 0:6], X[:, tt * D:tt * D + 512]), waits, sdeps=sdeps)
            tr.add("vector", lambda e, tt=tt: e.bn_stats(
                BST[:, 6:12], X[:, tt * D + 512:tt * D + 1024]), [])
            ag = tr.add("vector", lambda e, tt=tt: e.bn_aggr(
                STATS[:, 2 * tt:2 * tt + 2], BST[:]), [], sdeps=[s1])
            t1 = tr.add("scalar", lambda e, tt=tt: e.activation(
                STATS[:, 32 + tt:33 + tt], STATS[:, 2 * tt + 1:2 * tt + 2],
                AF.Identity, bias=EPSC[:, 0:1]), [E(ag)])
            t2 = tr.add("vector", lambda e, tt=tt: e.reciprocal(
                STATS[:, 48 + tt:49 + tt], STATS[:, 32 + tt:33 + tt]), [E(t1)],
                sdeps=[ag])
            t3 = tr.add("scalar", lambda e, tt=tt: e.activation(
                STATS[:, 32 + tt:33 + tt], STATS[:, 48 + tt:49 + tt], AF.Sqrt),
                [E(t2)], sdeps=[t1])
            t4 = tr.add("vector", lambda e, tt=tt: e.scalar_tensor_tensor(
                STATS[:, 48 + tt:49 + tt], STATS[:, 2 * tt:2 * tt + 1],
                NEGONE[:, 0:1], STATS[:, 32 + tt:33 + tt], ALU.mult, ALU.mult),
                [E(t3)], sdeps=[t2])
            ap_op = tr.add("scalar", lambda e, tt=tt, out_ap=out_ap: e.activation(
                out_ap, X[:, tt * D:(tt + 1) * D], AF.Identity,
                bias=STATS[:, 48 + tt:49 + tt], scale=STATS[:, 32 + tt:33 + tt]),
                [E(t4)], sdeps=[t3])
            return ap_op

        ffn_state = {"m1dma": {}, "m2dma": {}, "m1cur": None, "m2cur": None}

        def ffn(x_ready, arena_free, xread_sdep):
            pass

            round_ops = []
            prev_r1 = None
            for tt in range(NT):
                lnslot = (tt % 2) * D
                lnw = list(x_ready) + list(arena_free) if tt == 0 else []
                ap_op = layernorm_tile(
                    tt, LN1[:, lnslot:lnslot + D], lnw,
                    sdeps=([round_ops[-1]] if tt >= 2 else
                           ([xread_sdep] if tt == 0 else [])))
                am = tr.add("vector", lambda e, tt=tt, lnslot=lnslot: e.tensor_reduce(
                    AMAX1[:, tt:tt + 1], LN1[:, lnslot:lnslot + D], AX.X, ALU.max,
                    apply_absolute_value=True), [E(ap_op)])
                c1 = tr.add("vector", lambda e, tt=tt: e.tensor_scalar(
                    SCR16[:, tt:tt + 1], AMAX1[:, tt:tt + 1], 1e-5, None, ALU.max),
                    [], sdeps=[am])
                c2 = tr.add("vector", lambda e, tt=tt: e.reciprocal(
                    S1C[:, tt:tt + 1], SCR16[:, tt:tt + 1]), [], sdeps=[c1])
                c3 = tr.add("vector", lambda e, tt=tt: e.tensor_scalar(
                    S1C[:, tt:tt + 1], S1C[:, tt:tt + 1], 127.0, None, ALU.mult),
                    [], sdeps=[c2])
                qrs = 2048 + (tt % 2) * D
                r1 = tr.add("scalar", lambda e, tt=tt, lnslot=lnslot, qrs=qrs:
                            e.activation(
                    QRL[:, qrs:qrs + D], LN1[:, lnslot:lnslot + D], AF.Identity,
                    bias=RCCOL[:, 0:1], scale=S1C[:, tt:tt + 1]),
                    [E(c3), E(op_consts)], sdeps=[prev_r1])
                prev_r1 = r1
                r2 = tr.add("vector", lambda e, tt=tt, qrs=qrs: e.tensor_scalar(
                    XQ[:].rearrange("p (dt n c) -> p dt n c", dt=NDT, n=NT)[:, :, tt, :],
                    QRL[:, qrs:qrs + D].rearrange("p (dt c) -> p dt c", c=128),
                    -RC, None, ALU.add),
                    [E(r1)] + (list(arena_free) if tt == 0 else []))
                round_ops.append(r2)
            f1a = tr.add("vector", lambda e: e.reciprocal(F1C[:], S1C[:]), [],
                         sdeps=[round_ops[-1]])
            f1b_op = tr.add("vector", lambda e: e.tensor_scalar(
                F1C[:], F1C[:], WSC[:, 0:1], None, ALU.mult),
                [w1_done, E(w1_cinv)], sdeps=[f1a])
            f1row_ops = []
            prev_cp = None
            for tt in range(NT):
                tp = tr.add("tensor", lambda e, tt=tt: e.transpose(
                    PD[0:1, 0:128], F1C[:, tt:tt + 1], IDF[:]),
                    [E(f1b_op), E(ld_id), E(prev_cp)])
                prev_cp = tr.add("scalar", lambda e, tt=tt: e.copy(
                    F1ROW[0:1, tt * 128:(tt + 1) * 128], PD[0:1, 0:128]), [E(tp)])
                f1row_ops.append(prev_cp)
            tp_ops = []
            for dt in range(NDT):
                tp_ops.append(tr.dma("scalar", lambda e, dt=dt: e.dma_start_transpose(
                    XQT[:, dt * S:(dt + 1) * S].rearrange("p (n f) -> p n f", n=NT),
                    XQ[:, dt * S:(dt + 1) * S].rearrange("p (n f) -> p n f", n=NT)),
                    sem_tp[dt % 2], [E(round_ops[-1])] + list(arena_free)))
            xqt_done = [E(t) for t in tp_ops]

            st = {"evac": {}, "silu_last": {}, "ra_last": {}, "round_last": {},
                  "m2stop": {}, "evr": {}, "sbc": None, "fcp": None,
                  "red_prev": None, "s2rowcp": None, "g_last": None,
                  "m2clear": None, "tr_par": {}, "tl_par": {}}
            all_evr = []

            def emit_m1_header(w):
                cin0 = max(WIN * w - 1, 0)
                wid = min(WIN * w + WIN + 1, S) - cin0
                fm = tr.add("tensor", lambda e, cin0=cin0, wid=wid: e.matmul(
                    PC[:, 0:wid], ONEFR[0:1, :], F1ROW[0:1, cin0:cin0 + wid],
                    start=True, stop=True),
                    [E(f1row_ops[-1]), E(st["sbc"])])
                st["fcp"] = tr.add("scalar", lambda e, wid=wid: e.copy(
                    F1B[:, 0:wid], PC[:, 0:wid]), [E(fm)],
                    sdeps=[st["s2rowcp"]])

            def emit_m1_ct(w, ct):
                cin0 = max(WIN * w - 1, 0)
                wid = min(WIN * w + WIN + 1, S) - cin0
                off = WIN * w - cin0
                jl0 = 1 if w == 0 else 0
                jr1 = WIN - 1 if w == NWIN - 1 else WIN
                g = ct // 2
                if ct % 2 == 0:
                    bufsel = g % 2
                    dw = [ffn_state["m1dma"].get(bufsel)]
                    if w == 0 and ct == 0:
                        dw += [w1_done] + list(x_ready)
                    ffn_state["m1cur"] = tr.dma(
                        "sync", lambda e, g=g, bufsel=bufsel: e.dma_start(
                            WQ1B[:, bufsel * 2 * D:(bufsel + 1) * 2 * D]
                            .rearrange("p (ct c) -> p ct c", c=D),
                            wq1t[2 * g:2 * g + 2].rearrange("ct p c -> p ct c")),
                        sem_m1[g % 3], dw)
                cur = ffn_state["m1cur"]
                lct = ct % 2
                bufsel = g % 2
                h1ps = [PS1[:, 0:512], PS1[:, 512:1024],
                        PS2[:, 0:512], PS2[:, 512:1024]][ct % 4]
                pslot = ct % 4
                mm = None
                for dt in range(NDT):
                    waits = []
                    if dt == 0:
                        waits = [("d", cur.dma_sem, cur.post, cur),
                                 E(st["evac"].get(pslot))]
                        if ct % 4 >= 2:
                            waits.append(E(st["m2clear"]))
                        if w == 0 and ct <= 1:
                            waits = waits + xqt_done
                    mm = tr.add("tensor", lambda e, dt=dt, cin0=cin0, wid=wid,
                                h1ps=h1ps, bufsel=bufsel, lct=lct: e.matmul(
                        h1ps[:, 0:wid],
                        WQ1B[:, bufsel * 2 * D + lct * D + dt * 128:
                             bufsel * 2 * D + lct * D + (dt + 1) * 128],
                        XQT[:, dt * S + cin0:dt * S + cin0 + wid],
                        start=(dt == 0), stop=(dt == NDT - 1)), waits)
                if ct % 2 == 1:
                    ffn_state["m1dma"][bufsel] = E(mm)
                hslot = (ct % 2) * (WIN + 2)
                ev = tr.add("vector", lambda e, wid=wid, h1ps=h1ps, hslot=hslot:
                            e.tensor_mul(HS[:, hslot:hslot + wid],
                                         h1ps[:, 0:wid], F1B[:, 0:wid]),
                            [E(mm), E(st["fcp"]),
                             E(st["tr_par"].get(ct % 2))])
                st["evac"][pslot] = ev
                taslot = (ct % 2) * WIN
                cwc = ct * 3
                tR = tr.add("scalar", lambda e, hslot=hslot, taslot=taslot,
                            jr1=jr1, off=off, cwc=cwc: e.activation(
                    TA[:, taslot:taslot + jr1],
                    HS[:, hslot + off + 1:hslot + off + 1 + jr1],
                    AF.Identity, scale=CW[:, cwc + 2:cwc + 3]),
                    [E(ev), E(ld_cw), E(st["tl_par"].get(ct % 2))])
                st["tr_par"][ct % 2] = tR
                after_tr = [tR]
                if jr1 < WIN:
                    z = tr.add("vector", lambda e, taslot=taslot, jr1=jr1: e.memset(
                        TA[:, taslot + jr1:taslot + WIN], 0.0), [E(tR)])
                    after_tr.append(z)
                tL = tr.add("vector", lambda e, hslot=hslot, taslot=taslot,
                            jl0=jl0, off=off, cwc=cwc: e.scalar_tensor_tensor(
                    TB[:, taslot + jl0:taslot + WIN],
                    HS[:, hslot + off + jl0 - 1:hslot + off + WIN - 1],
                    CW[:, cwc:cwc + 1],
                    TA[:, taslot + jl0:taslot + WIN], ALU.mult, ALU.add),
                    [E(o) for o in after_tr])
                if jl0 > 0:
                    tr.add("vector", lambda e, taslot=taslot: e.tensor_copy(
                        TB[:, taslot:taslot + 1], TA[:, taslot:taslot + 1]),
                        [E(tR)])
                st["tl_par"][ct % 2] = tL
                tM = tr.add("vector", lambda e, hslot=hslot, taslot=taslot,
                            off=off, cwc=cwc: e.scalar_tensor_tensor(
                    TB[:, taslot:taslot + WIN],
                    HS[:, hslot + off:hslot + off + WIN],
                    CW[:, cwc + 1:cwc + 2],
                    TB[:, taslot:taslot + WIN], ALU.mult, ALU.add), [],
                    sdeps=[tL])
                sl = tr.add("scalar", lambda e, ct=ct, taslot=taslot: e.activation(
                    H2[:, ct * WIN:(ct + 1) * WIN], TB[:, taslot:taslot + WIN],
                    AF.Silu, bias=CB[:, ct:ct + 1]),
                    [E(tM), E(ld_cb)],
                    sdeps=[st["ra_last"].get(w - 1), tR])
                st["silu_last"][w] = sl

            def emit_m2_ct(w, ct):
                psg = [PS2[:, 0:512], PS2[:, 512:1024], PA[:], PB[:]]
                g = ct // 2
                if ct % 2 == 0:
                    bufsel = g % 2
                    dw = [ffn_state["m2dma"].get(bufsel)]
                    if w == 0 and ct == 0:
                        dw += [w2_done] + list(x_ready)
                    ffn_state["m2cur"] = tr.dma(
                        "sync", lambda e, g=g, bufsel=bufsel: e.dma_start(
                            WQ2B[:, bufsel * 2 * D:(bufsel + 1) * 2 * D]
                            .rearrange("p (ct c) -> p ct c", c=D),
                            wq2t[2 * g:2 * g + 2].rearrange("ct p c -> p ct c")),
                        sem_m2[g % 3], dw)
                cur = ffn_state["m2cur"]
                lct = ct % 2
                bufsel = g % 2
                mm2 = None
                for tt in range(2):
                    for fb in range(2):
                        waits = []
                        if tt == 0 and fb == 0:
                            waits = [("d", cur.dma_sem, cur.post, cur)]
                            if ct == 0:
                                waits += [E(st["round_last"][w]),
                                          E(st["evac"].get(2)), E(st["evac"].get(3))]
                                waits += [E(o) for o in st["evr"].get(w - 2, [])]
                        mm2 = tr.add("tensor", lambda e, ct=ct, tt=tt, fb=fb,
                                     bufsel=bufsel, lct=lct: e.matmul(
                            psg[tt * 2 + fb],
                            XQ2[:, ct * WIN + tt * 128:ct * WIN + (tt + 1) * 128],
                            WQ2B[:, bufsel * 2 * D + lct * D + fb * 512:
                                 bufsel * 2 * D + lct * D + (fb + 1) * 512],
                            start=(ct == 0), stop=(ct == NCT - 1)), waits)
                if ct % 2 == 1:
                    ffn_state["m2dma"][bufsel] = E(mm2)
                if ct == NCT - 1:
                    st["m2stop"][w] = mm2

            def emit_m2_evacs(w):
                psg = [PS2[:, 0:512], PS2[:, 512:1024], PA[:], PB[:]]
                evrs = []
                prev = None
                for tt in range(2):
                    for fb in range(2):
                        gtt = 2 * w + tt
                        evr = tr.add("vector", lambda e, tt=tt, fb=fb, gtt=gtt:
                                     e.scalar_tensor_tensor(
                            X[:, gtt * D + fb * 512:gtt * D + (fb + 1) * 512],
                            psg[tt * 2 + fb], FAC2[:, gtt:gtt + 1],
                            X[:, gtt * D + fb * 512:gtt * D + (fb + 1) * 512],
                            ALU.mult, ALU.add),
                            [E(st["m2stop"][w]), E(st["fc2_%d" % w])],
                            sdeps=[st["fc2_%d" % w]] if prev is None else [])
                        prev = evr
                        evrs.append(evr)
                st["evr"][w] = evrs
                st["m2clear"] = evrs[1]
                all_evr.extend(evrs)

            def emit_tail(w):
                red = tr.add("vector", lambda e: e.tensor_reduce(
                    ACC[:], H2[:].rearrange("p (ct j) -> p j ct", j=WIN), AX.X,
                    ALU.max, apply_absolute_value=True),
                    [E(st["silu_last"][w])], sdeps=[st["red_prev"]])
                st["red_prev"] = red
                prev = red
                for i in range(2):
                    tp = tr.add("tensor", lambda e, i=i: e.transpose(
                        PD[:, 0:128], ACC[:, i * 128:(i + 1) * 128], IDF[:]),
                        [E(prev)])
                    rd = tr.add("vector", lambda e, i=i, w=w: e.tensor_reduce(
                        AMAX2[:, 2 * w + i:2 * w + i + 1], PD[:, 0:128], AX.X,
                        ALU.max), [E(tp)], sdeps=[red])
                    prev = rd
                s2a = tr.add("vector", lambda e, w=w: e.tensor_scalar(
                    SCR16[:, 0:2], AMAX2[:, 2 * w:2 * w + 2], 1e-5, None, ALU.max),
                    [], sdeps=[prev])
                s2b_ = tr.add("vector", lambda e, w=w: e.reciprocal(
                    S2CC[:, 2 * w:2 * w + 2], SCR16[:, 0:2]), [], sdeps=[s2a])
                s2c = tr.add("vector", lambda e, w=w: e.tensor_scalar(
                    S2CC[:, 2 * w:2 * w + 2], S2CC[:, 2 * w:2 * w + 2], 127.0, None,
                    ALU.mult), [], sdeps=[s2b_])
                fc1 = tr.add("vector", lambda e, w=w: e.reciprocal(
                    FAC2[:, 2 * w:2 * w + 2], S2CC[:, 2 * w:2 * w + 2]), [],
                    sdeps=[s2c])
                st["fc2_%d" % w] = tr.add("vector", lambda e, w=w: e.tensor_scalar(
                    FAC2[:, 2 * w:2 * w + 2], FAC2[:, 2 * w:2 * w + 2], WSC[:, 1:2],
                    None, ALU.mult), [w2_done, E(w2_cinv)], sdeps=[fc1])
                rowcp = st["s2rowcp"]
                for i in range(2):
                    tps2 = tr.add("tensor", lambda e, i=i, w=w: e.transpose(
                        PD[0:1, 0:128], S2CC[:, 2 * w + i:2 * w + i + 1], IDF[:]),
                        [E(s2c), E(rowcp)])
                    rowcp = tr.add("scalar", lambda e, i=i: e.copy(
                        S2ROW[0:1, i * 128:(i + 1) * 128], PD[0:1, 0:128]),
                        [E(tps2)], sdeps=[st["fcp"]] if i == 0 else [])
                st["s2rowcp"] = rowcp
                sbm = tr.add("tensor", lambda e: e.matmul(
                    PC[:, 0:WIN], ONEFR[0:1, :], S2ROW[0:1, 0:WIN], start=True,
                    stop=True), [E(rowcp), E(st["fcp"])])
                sbc = tr.add("scalar", lambda e: e.copy(S2B[:], PC[:, 0:WIN]),
                             [E(sbm)], sdeps=[rowcp])
                st["sbc"] = sbc
                ra = None
                rbs = []
                mus = []
                for ct in range(NCT):
                    mu = tr.add("gpsimd", lambda e, ct=ct: e.tensor_mul(
                        H2[:, ct * WIN:(ct + 1) * WIN],
                        H2[:, ct * WIN:(ct + 1) * WIN], S2B[:]),
                        [E(sbc)] if ct == 0 else [],
                        sdeps=[st["g_last"]] if ct == 0 else [])
                    if ct == 0:
                        st["g_last"] = None
                    mus.append(mu)
                for ct in range(NCT):
                    qs = (ct % 2) * WIN
                    ra = tr.add("scalar", lambda e, ct=ct, qs=qs: e.activation(
                        QR[:, qs:qs + WIN], H2[:, ct * WIN:(ct + 1) * WIN],
                        AF.Identity, bias=RCCOL[:, 0:1]),
                        [E(mus[ct])] + ([E(rbs[ct - 2])] if ct >= 2 else []),
                        sdeps=[st["silu_last"][w] if ct == 0 else None])
                    rb = tr.add("gpsimd", lambda e, ct=ct, qs=qs: e.tensor_add(
                        XQ2[:, ct * WIN:(ct + 1) * WIN], QR[:, qs:qs + WIN],
                        NRCB[:]),
                        [E(ra), E(op_nrcb)]
                        + ([E(st["m2stop"].get(w - 1))] if ct == 0 else []))
                    rbs.append(rb)
                st["ra_last"][w] = ra
                st["round_last"][w] = rbs[-1]

            for w in range(NWIN + 1):
                if w < NWIN:
                    emit_m1_header(w)
                    for ct in range(NCT):
                        emit_m1_ct(w, ct)
                if w >= 1:
                    for ct in range(NCT):
                        emit_m2_ct(w - 1, ct)
                    emit_m2_evacs(w - 1)
                if w < NWIN:
                    emit_tail(w)
            return all_evr

        at_state = {"exp_prev": {0: None, 1: None}, "av_prev": None,
                    "pair_evacs": None,
                    "pe_prev": None, "pe_bank": {0: None, 1: None},
                    "bc_prev": None, "shift_prev": None,
                    "sq_guard": None, "s2_guard": None, "mx_prev": None,
                    "rowb_guard": None, "xread_guard": None}

        def attn(x_ready, arena_free, xread_sdep):
            ln_last = None
            for tt in range(NT):
                ln_last = layernorm_tile(
                    tt, L2TOK[:].rearrange("p (dt n c) -> p dt n c", dt=NDT, n=NT)
                    [:, :, tt, :],
                    (list(x_ready) + list(arena_free)) if tt == 0 else [],
                    sdeps=[xread_sdep] if tt == 0 else [])
            ltp = []
            for dt in range(NDT):
                ltp.append(tr.dma("scalar", lambda e, dt=dt: e.dma_start_transpose(
                    L2T[:, dt * S:(dt + 1) * S].rearrange("p (n f) -> p n f", n=NT),
                    L2TOK[:, dt * S:(dt + 1) * S].rearrange("p (n f) -> p n f", n=NT)),
                    sem_tp[dt % 2], [E(ln_last)] + list(arena_free)))
            l2t_done = [E(t) for t in ltp]

            preadd = []
            pa_prev = None
            for tt in range(NT):
                pa_ = tr.add("vector", lambda e, tt=tt: e.tensor_add(
                    X[:, tt * D:(tt + 1) * D], X[:, tt * D:(tt + 1) * D], BOUTB[:]),
                    [E(ln_last), E(op_bias_b)] if tt == 0 else [],
                    sdeps=[xread_sdep] if tt == 0 else [])
                preadd.append(pa_)
                pa_prev = pa_

            last_proj = []
            for hg in range(NHG):
                guard = l2t_done if hg == 0 else [E(o) for o in last_proj[-8:]]
                kt_last_evac = None
                ksq_prev = None
                kw_consumed = None
                k2m_init = tr.add("vector", lambda e: e.memset(ROWS[0:1, 0:16], 0.0),
                                  guard,
                                  sdeps=[at_state["mx_prev"], pa_prev])
                for fp in range(FPG):
                    pair = hg * FPG + fp
                    kw = tr.dma("gpsimd", lambda e, pair=pair: e.dma_start(
                        WQ1B[:, 0:D].rearrange("p (dt c) -> p dt c", c=128),
                        w_qkv.rearrange("(dt p) c -> p dt c", p=128)
                        [:, :, D + pair * 128:D + (pair + 1) * 128]), sem_qk[0],
                        guard + [E(kw_consumed)])
                    km = None
                    for Nb in range(4):
                        for dt in range(NDT):
                            waits = []
                            if dt == 0:
                                waits = [("d", kw.dma_sem, kw.post, kw),
                                         E(kt_last_evac)]
                                if Nb == 0 and fp == 0:
                                    waits = waits + guard
                            km = tr.add("tensor", lambda e, fp=fp, Nb=Nb, dt=dt:
                                        e.matmul(
                                PC[:, 0:512], WQ1B[:, dt * 128:(dt + 1) * 128],
                                L2T[:, dt * S + Nb * 512:dt * S + (Nb + 1) * 512],
                                start=(dt == 0), stop=(dt == NDT - 1)), waits)
                        ke = tr.add("scalar", lambda e, fp=fp, Nb=Nb, pair=pair:
                                    e.activation(
                            KT[:, fp * S + Nb * 512:fp * S + (Nb + 1) * 512],
                            PC[:, 0:512], AF.Identity,
                            bias=BQCOL[:, 8 + pair:9 + pair]), [E(km), E(ld_bq)])
                        kt_last_evac = ke
                        sq = tr.add("vector", lambda e, fp=fp, Nb=Nb: e.tensor_mul(
                            QSQ[:], KT[:, fp * S + Nb * 512:fp * S + (Nb + 1) * 512],
                            KT[:, fp * S + Nb * 512:fp * S + (Nb + 1) * 512]),
                            [E(ke)], sdeps=[at_state["sq_guard"]])
                        prev_rd = None
                        for hh in range(2):
                            qm = tr.add("tensor", lambda e, hh=hh: e.matmul(
                                PD[0:1, 0:512], ONEBC[hh * 64:hh * 64 + 64, 0:1],
                                QSQ[hh * 64:hh * 64 + 64, :], start=True, stop=True),
                                [E(sq), E(at_state["pe_bank"][0]), E(prev_rd)])
                            rd = tr.add("vector", lambda e: e.tensor_reduce(
                                ROWS[0:1, 32:33], PD[0:1, 0:512], AX.X, ALU.max),
                                [E(qm)], sdeps=[sq, ksq_prev])
                            prev_rd = rd
                            mx = tr.add("vector", lambda e, fp=fp, hh=hh:
                                        e.tensor_tensor(
                                ROWS[0:1, 2 * fp + hh:2 * fp + hh + 1],
                                ROWS[0:1, 2 * fp + hh:2 * fp + hh + 1],
                                ROWS[0:1, 32:33], ALU.max), [E(k2m_init)],
                                sdeps=[rd])
                            ksq_prev = mx
                            at_state["sq_guard"] = mx
                    kw_consumed = ke
                at_state["mx_prev"] = ksq_prev
                mk1 = tr.add("scalar", lambda e: e.activation(
                    ROWS[0:1, 16:24], ROWS[0:1, 0:8], AF.Ln), [E(ksq_prev)])
                mk2 = tr.add("scalar", lambda e: e.activation(
                    ROWS[0:1, 16:24], ROWS[0:1, 16:24], AF.Exp, scale=0.5), [],
                    sdeps=[mk1])

                vw = tr.dma("gpsimd", lambda e, hg=hg: e.dma_start(
                    WQ2B[:, 0:4 * D].rearrange("p (dt c) -> p dt c", c=512),
                    w_qkv.rearrange("(dt p) c -> p dt c", p=128)
                    [:, :, 2 * D + hg * 512:2 * D + (hg + 1) * 512]), sem_qk[1],
                    [E(kt_last_evac)] + guard)
                vones = tr.add("vector", lambda e: e.memset(
                    ARBB[:, 0:NKT * VBLK].rearrange("p (t c) -> p t c", c=VBLK)
                    [:, :, 64:VBLK:65], 1.0), guard, sdeps=[at_state["mx_prev"]])
                v_evacs = []
                for tt in range(NKT):
                    vm = None
                    for dt in range(NDT):
                        waits = []
                        if dt == 0:
                            waits = [("d", vw.dma_sem, vw.post, vw), E(vones),
                                     E(ksq_prev)]
                            if tt >= 2:
                                waits.append(E(v_evacs[tt - 2]))
                        vm = tr.add("tensor", lambda e, tt=tt, dt=dt: e.matmul(
                            PC[:, 0:512] if tt % 2 == 0 else PD[:, 0:512],
                            L2T[:, dt * S + tt * 128:dt * S + (tt + 1) * 128],
                            WQ2B[:, dt * 512:(dt + 1) * 512],
                            start=(dt == 0), stop=(dt == NDT - 1)), waits)
                    ve = tr.add("vector", lambda e, tt=tt, hg=hg: e.tensor_add(
                        ARBB[:, tt * VBLK:(tt + 1) * VBLK]
                        .rearrange("p (h c) -> p h c", c=65)[:, :, 0:64],
                        (PC[:, 0:512] if tt % 2 == 0 else PD[:, 0:512])
                        .rearrange("p (h c) -> p h c", c=64),
                        BQVB[:, hg * 512:(hg + 1) * 512]
                        .rearrange("p (h c) -> p h c", c=64)), [E(vm)],
                        sdeps=[vones] if tt == 0 else [])
                    v_evacs.append(ve)

                qw_consumed = None
                wo_dma = None
                for qb in range(NQB):
                    qt_last = None
                    negb_last = None
                    for fp in range(FPG):
                        pair = hg * FPG + fp
                        qw = tr.dma("gpsimd", lambda e, pair=pair: e.dma_start(
                            WQ1B[:, D:2 * D].rearrange("p (dt c) -> p dt c", c=128),
                            w_qkv.rearrange("(dt p) c -> p dt c", p=128)
                            [:, :, pair * 128:(pair + 1) * 128]), sem_qk[0],
                            [E(qw_consumed), E(v_evacs[-1])])
                        qm = None
                        for dt in range(NDT):
                            waits = []
                            if dt == 0:
                                waits = [("d", qw.dma_sem, qw.post, qw), E(qt_last),
                                         E(v_evacs[-1]), E(at_state["pe_prev"])]
                            qm = tr.add("tensor", lambda e, fp=fp, dt=dt, qb=qb:
                                        e.matmul(
                                PC[:, 0:512], WQ1B[:, D + dt * 128:D + (dt + 1) * 128],
                                L2T[:, dt * S + qb * 512:dt * S + (qb + 1) * 512],
                                start=(dt == 0), stop=(dt == NDT - 1)), waits)
                        qw_consumed = qm
                        qe = tr.add("scalar", lambda e, fp=fp, pair=pair: e.activation(
                            ARBB[:, QT_OFF + fp * 512:QT_OFF + (fp + 1) * 512],
                            PC[:, 0:512], AF.Identity,
                            bias=BQCOL[:, pair:pair + 1]), [E(qm)])
                        qt_last = qe
                        sq = tr.add("vector", lambda e, fp=fp: e.tensor_mul(
                            QSQ[:], ARBB[:, QT_OFF + fp * 512:QT_OFF + (fp + 1) * 512],
                            ARBB[:, QT_OFF + fp * 512:QT_OFF + (fp + 1) * 512]),
                            [E(qe)], sdeps=[at_state["sq_guard"]])
                        prev_rd = None
                        for hh in range(2):
                            idx = 2 * fp + hh
                            qm2 = tr.add("tensor", lambda e, hh=hh: e.matmul(
                                PD[0:1, 0:512], ONEBC[hh * 64:hh * 64 + 64, 0:1],
                                QSQ[hh * 64:hh * 64 + 64, :], start=True, stop=True),
                                [E(sq), E(at_state["pe_bank"][0]), E(prev_rd),
                                 E(at_state["s2_guard"])])
                            rd = tr.add("vector", lambda e, idx=idx: e.tensor_reduce(
                                ROWS[0:1, 40 + idx:41 + idx], PD[0:1, 0:512], AX.X,
                                ALU.max), [E(qm2)], sdeps=[sq])
                            prev_rd = rd
                            at_state["sq_guard"] = rd
                            negb_last = rd
                    bl1 = tr.add("scalar", lambda e: e.activation(
                        ROWS[0:1, 48:56], ROWS[0:1, 40:48], AF.Ln), [E(negb_last)],
                        sdeps=[at_state["s2_guard"]])
                    bl2 = tr.add("scalar", lambda e: e.activation(
                        ROWS[0:1, 48:56], ROWS[0:1, 48:56], AF.Exp, scale=0.5), [],
                        sdeps=[bl1])
                    at_state["s2_guard"] = bl2
                    bl3 = tr.add("vector", lambda e: e.tensor_mul(
                        ROWS[0:1, 40:48], ROWS[0:1, 48:56], ROWS[0:1, 16:24]),
                        [E(bl2), E(mk2)], sdeps=[negb_last])
                    bl4 = tr.add("vector", lambda e: e.tensor_scalar(
                        ROWS[0:1, 40:48], ROWS[0:1, 40:48], -0.125, None, ALU.mult),
                        [], sdeps=[bl3])
                    nbm = tr.add("tensor", lambda e: e.matmul(
                        PC[:, 0:8], ONEFR[0:1, :], ROWS[0:1, 40:48], start=True,
                        stop=True), [E(bl4), E(qt_last)])
                    nbc = tr.add("scalar", lambda e: e.copy(NEGB[:], PC[:, 0:8]),
                                 [E(nbm)], sdeps=[qt_last])

                    for fp in range(FPG):
                        av_done = {}
                        exp_ops = {}

                        def emit_av(g, fp=fp, exp_ops=exp_ops):
                            eset = g % 2
                            av = None
                            for kk in range(2):
                                kt = 2 * g + kk
                                for hh in range(2):
                                    waits = []
                                    if kk == 0 and hh == 0:
                                        waits = [E(exp_ops[(g, 0)]),
                                                 E(exp_ops[(g, 1)])]
                                        if g == 0 and at_state["pair_evacs"]:
                                            waits += at_state["pair_evacs"]
                                    av = tr.add("tensor", lambda e, fp=fp, kt=kt,
                                                hh=hh, eset=eset, kk=kk: e.matmul(
                                        (PA if hh == 0 else PB)[0:65, :],
                                        ARBB[:, kt * VBLK + (2 * fp + hh) * 65:
                                             kt * VBLK + (2 * fp + hh) * 65 + 65],
                                        ARBB[:, E_OFF + eset * 2048 + hh * 1024 + kk * 512:
                                             E_OFF + eset * 2048 + hh * 1024 + (kk + 1) * 512],
                                        start=(kt == 0), stop=(kt == NKT - 1)), waits)
                            return av

                        for g in range(8):
                            eset = g % 2
                            for hh in range(2):
                                sc_last = None
                                for kk in range(2):
                                    kt = 2 * g + kk
                                    waits = []
                                    if kk == 0:
                                        if g >= 1:
                                            waits = [E(exp_ops[(g - 1, hh)])]
                                        else:
                                            waits = [E(nbc),
                                                     E(at_state["exp_prev"][hh])]
                                    sc_last = tr.add("tensor", lambda e, fp=fp,
                                                     kt=kt, hh=hh, kk=kk: e.matmul(
                                        (PS1 if hh == 0 else PS2)
                                        [:, kk * 512:(kk + 1) * 512],
                                        KT[hh * 64:hh * 64 + 64,
                                           fp * S + kt * 128:fp * S + (kt + 1) * 128],
                                        ARBB[hh * 64:hh * 64 + 64,
                                             QT_OFF + fp * 512:QT_OFF + (fp + 1) * 512],
                                        start=True, stop=True,
                                        tile_position=(hh * 64, 0)), waits)
                                ew = [E(sc_last)]
                                if g >= 2:
                                    ew.append(E(av_done[g - 2]))
                                elif at_state["av_prev"] is not None:
                                    ew.append(at_state["av_prev"])
                                exp_ops[(g, hh)] = tr.add(
                                    "scalar", lambda e, fp=fp, hh=hh, eset=eset:
                                    e.activation(
                                        ARBB[:, E_OFF + eset * 2048 + hh * 1024:
                                             E_OFF + eset * 2048 + (hh + 1) * 1024],
                                        (PS1 if hh == 0 else PS2)[:],
                                        AF.Exp, scale=0.125,
                                        bias=NEGB[:, 2 * fp + hh:2 * fp + hh + 1]),
                                    ew, sdeps=[nbc] if (g == 0 and hh == 0) else [])
                            if g >= 1:
                                av_done[g - 1] = emit_av(g - 1)
                        av = av_done[7] = emit_av(7)
                        at_state["exp_prev"] = {0: exp_ops[(7, 0)],
                                                1: exp_ops[(7, 1)]}
                        at_state["av_prev"] = E(av)

                        pair_evacs = []
                        shift_ev = None
                        for hh in range(2):
                            ps = PA if hh == 0 else PB
                            rc_ = tr.add("vector", lambda e, ps=ps: e.reciprocal(
                                ROWR[0:1, :], ps[64:65, :]), [E(av)],
                                sdeps=[at_state["rowb_guard"]])
                            rb_ = tr.add("scalar", lambda e: e.copy(
                                ROWB[0:1, :], ROWR[0:1, :]), [E(rc_)],
                                sdeps=[at_state["bc_prev"]])
                            bm = tr.add("tensor", lambda e: e.matmul(
                                PC[0:64, :], ONEBR[0:1, 0:64], ROWB[0:1, :],
                                start=True, stop=True),
                                [E(rb_), E(at_state["bc_prev"])])
                            bc_ = tr.add("scalar", lambda e: e.copy(
                                BCS[:], PC[0:64, :]), [E(bm)], sdeps=[rb_])
                            at_state["bc_prev"] = bc_
                            if hh == 0:
                                mu = tr.add("vector", lambda e, fp=fp, ps=ps:
                                            e.tensor_mul(
                                    AVT[0:64, fp * QB:(fp + 1) * QB], ps[0:64, :],
                                    BCS[:]), [E(bc_)], sdeps=[rc_])
                            else:
                                mu = tr.add("vector", lambda e, ps=ps: e.tensor_mul(
                                    AVST[:], ps[0:64, :], BCS[:]),
                                    [E(bc_), at_state["shift_prev"]],
                                    sdeps=[rc_])
                                shift_ev = tr.dma("gpsimd", lambda e, fp=fp:
                                                  e.dma_start(
                                    AVT[64:128, fp * QB:(fp + 1) * QB], AVST[:]),
                                    sem_sh, [E(mu)])
                                at_state["shift_prev"] = (
                                    "d", shift_ev.dma_sem, shift_ev.post, shift_ev)
                            at_state["rowb_guard"] = bc_
                            pair_evacs.append(E(mu))
                        if shift_ev is not None:
                            pair_evacs.append(("d", shift_ev.dma_sem, shift_ev.post,
                                               shift_ev))
                        at_state["pair_evacs"] = pair_evacs

                    if qb == 0:
                        wo_dma = tr.dma("gpsimd", lambda e, hg=hg: e.dma_start(
                            WQ2B[:, 4 * D:8 * D].rearrange("p (dt c) -> p dt c", c=D),
                            w_out.rearrange("(dt p) c -> p dt c", p=128)
                            [:, hg * 4:hg * 4 + 4, :]), sem_qk[1],
                            [E(v_evacs[-1])])
                    proj_pes = {}
                    for tt in range(4):
                        for fb in range(2):
                            gi = tt * 2 + fb
                            pp = PD if gi % 2 == 0 else PC
                            pm = None
                            for fp in range(FPG):
                                waits = []
                                if fp == 0:
                                    waits = [("d", wo_dma.dma_sem, wo_dma.post,
                                              wo_dma)]
                                    if gi >= 2:
                                        waits.append(E(proj_pes[gi - 2]))
                                    else:
                                        waits.append(E(at_state["pe_bank"][gi]))
                                        waits += at_state["pair_evacs"]
                                pm = tr.add("tensor", lambda e, tt=tt, fb=fb,
                                            fp=fp, pp=pp: e.matmul(
                                    pp[:, 0:512],
                                    AVT[:, fp * QB + tt * 128:fp * QB + (tt + 1) * 128],
                                    WQ2B[:, 4 * D + fp * D + fb * 512:
                                         4 * D + fp * D + (fb + 1) * 512],
                                    start=(fp == 0), stop=(fp == FPG - 1)), waits)
                            gtt = qb * 4 + tt
                            pe = tr.add("vector", lambda e, gtt=gtt, fb=fb, pp=pp:
                                        e.scalar_tensor_tensor(
                                X[:, gtt * D + fb * 512:gtt * D + (fb + 1) * 512],
                                pp[:, 0:512], ONEF[:, 0:1],
                                X[:, gtt * D + fb * 512:gtt * D + (fb + 1) * 512],
                                ALU.mult, ALU.add),
                                [E(pm)], sdeps=[preadd[gtt]])
                            proj_pes[gi] = pe
                            at_state["pe_prev"] = pe
                            last_proj.append(pe)
                    at_state["pe_bank"] = {0: proj_pes[6], 1: proj_pes[7]}
            at_state["xread_guard"] = last_proj[-1]
            return last_proj

        xread_sdep = None
        for b in range(nb):
            xw = [] if b == 0 else [("d", sem_y, sem_y.count * 16, None)]
            for tt in range(NT):
                tr.dma("sync", lambda e, b=b, tt=tt: e.dma_start(
                    X[:, tt * D:(tt + 1) * D],
                    x_in.rearrange("(n p) c -> n p c", p=128)[b * NT + tt]),
                    sem_x, xw if tt == 0 else [])
            x_ready = [("d", sem_x, sem_x.count * 16, None)]
            arena_free = list(prep_done) if b == 0 else []
            if "f1" in phases:
                evs = ffn(x_ready, arena_free, xread_sdep)
                x_ready = [E(o) for o in evs[-8:]]
                xread_sdep = evs[-1]
                arena_free = []
            if "at" in phases:
                evs = attn(x_ready, arena_free, xread_sdep)
                x_ready = [E(o) for o in evs[-8:]]
                xread_sdep = evs[-1]
                arena_free = []
            if "f2" in phases:
                evs = ffn(x_ready, arena_free, xread_sdep)
                x_ready = [E(o) for o in evs[-8:]]
                xread_sdep = evs[-1]
            for tt in range(NT):
                tr.dma("sync", lambda e, b=b, tt=tt: e.dma_start(
                    y_out.rearrange("(n p) c -> n p c", p=128)[b * NT + tt],
                    X[:, tt * D:(tt + 1) * D]), sem_y, x_ready if tt == 0 else [])

        tr.add("sync", lambda e: e.nop(), [("d", sem_y, sem_y.count * 16, None)])
        tr.emit(nc, ctx)
    nc._dbg_ndrains = tr.n_drains
    return nc


_cache = {}


def _get_program(phases=("f1", "at", "f2")):
    key = tuple(phases)
    if key not in _cache:
        _cache[key] = build_program(phases)
    return _cache[key]


def _host_layouts(L0_attn_w_qkv, L0_attn_b_qkv, L0_attn_w_out, L0_attn_b_out,
                  L0_ffn_w1, L0_ffn_conv_w, L0_ffn_conv_b, L0_ffn_w2):
    wq = np.ascontiguousarray(np.asarray(L0_attn_w_qkv, np.float32))
    bq = np.asarray(L0_attn_b_qkv, np.float32).reshape(3 * D)
    bqcol = np.ascontiguousarray(bq[:3072].reshape(24, 128).T)
    bqv_row = np.ascontiguousarray(bq[2 * D:3 * D].reshape(1, D))
    bout_row = np.ascontiguousarray(np.asarray(L0_attn_b_out, np.float32).reshape(1, D))
    wo = np.ascontiguousarray(np.asarray(L0_attn_w_out, np.float32))
    w1 = np.ascontiguousarray(np.asarray(L0_ffn_w1, np.float32))
    cw = np.asarray(L0_ffn_conv_w, np.float32).reshape(HID, 3)
    cw_pt = np.ascontiguousarray(cw.reshape(NCT, 128, 3).transpose(1, 0, 2)
                                 .reshape(128, NCT * 3))
    cb = np.asarray(L0_ffn_conv_b, np.float32).reshape(HID)
    cb_pt = np.ascontiguousarray(cb.reshape(NCT, 128).T)
    w2 = np.ascontiguousarray(np.asarray(L0_ffn_w2, np.float32))
    return dict(w_qkv=wq, bqcol=bqcol, bqv_row=bqv_row, bout_row=bout_row,
                w_out=wo, w1=w1, cw_pt=cw_pt, cb_pt=cb_pt, w2=w2)


def kernel(L0_attn_x, L0_attn_w_qkv, L0_attn_b_qkv, L0_attn_w_out, L0_attn_b_out,
           L0_ffn_w1, L0_ffn_conv_w, L0_ffn_conv_b, L0_ffn_w2):
    x = np.ascontiguousarray(np.asarray(L0_attn_x, dtype=np.float32)).reshape(16 * S, D)
    shards = np.split(x, NCORES, axis=0)
    wmap = _host_layouts(L0_attn_w_qkv, L0_attn_b_qkv, L0_attn_w_out, L0_attn_b_out,
                         L0_ffn_w1, L0_ffn_conv_w, L0_ffn_conv_b, L0_ffn_w2)
    nc = _get_program()
    in_maps = [dict(x=s, **wmap) for s in shards]
    res = run_bass_kernel_spmd(nc, in_maps, core_ids=list(range(NCORES)))
    y = np.concatenate([r["y"] for r in res.results], axis=0)
    return y.reshape(16, S, D)
